# version 06
# baseline (speedup 1.0000x reference)
import contextlib
import numpy as np
import concourse.bass as bass
import concourse.mybir as mybir
from concourse.bass_utils import run_bass_kernel_spmd
from concourse.aot_env import allow_ici_all_to_all

F32 = mybir.dt.float32
BF16 = mybir.dt.bfloat16
I32 = mybir.dt.int32
AF = mybir.ActivationFunctionType
ALU = mybir.AluOpType

B = 2
L = 4096
H = 4096
DIN = 8192
NST = 16
KCV = 4
RDT = 256
ESSM = 288
NC = 8
DG = 2048
NDT = DG // 128
TCH = 512
NCH = L // TCH
NHK = H // 128
NDK = DIN // 128
EPS_LN = 1e-5
EPS_RMS = 1e-6
DBG = False
G4 = [[0, 1, 2, 3], [4, 5, 6, 7]]
G8 = [list(range(8))]
ENGS = ("sync", "vector", "scalar", "gpsimd", "tensor")


class Ctr:
    def __init__(self):
        self.v = {}

    def add(self, name, n):
        self.v[name] = self.v.get(name, 0) + n
        return self.v[name]

    def get(self, name):
        return self.v.get(name, 0)


def build_program():
    nc = bass.Bass("TRN2", target_bir_lowering=False)

    d_xTb = nc.dram_tensor("xTb", [H, L], F32, kind="ExternalInput").ap()
    d_xTs = nc.dram_tensor("xTs", [H, 1024], F32, kind="ExternalInput").ap()
    d_win = nc.dram_tensor("win", [H, 2 * DG], F32, kind="ExternalInput").ap()
    d_cw = nc.dram_tensor("cw", [128, NDT * KCV], F32, kind="ExternalInput").ap()
    d_cb = nc.dram_tensor("cb", [128, NDT], F32, kind="ExternalInput").ap()
    d_dtb = nc.dram_tensor("dtb", [128, NDT], F32, kind="ExternalInput").ap()
    d_dpar = nc.dram_tensor("dpar", [128, NDT], F32, kind="ExternalInput").ap()
    d_alog = nc.dram_tensor("alog", [128, NDT * NST], F32, kind="ExternalInput").ap()
    d_xpwT = nc.dram_tensor("xpwT", [128, NDT * ESSM], F32, kind="ExternalInput").ap()
    d_dtpwT = nc.dram_tensor("dtpwT", [128, 2 * DG], F32, kind="ExternalInput").ap()
    d_dtlnw = nc.dram_tensor("dtlnw", [128, 2], F32, kind="ExternalInput").ap()
    d_blnw = nc.dram_tensor("blnw", [NST, 1], F32, kind="ExternalInput").ap()
    d_clnw = nc.dram_tensor("clnw", [NST, 1], F32, kind="ExternalInput").ap()
    d_woT = nc.dram_tensor("woT", [DIN // 8, H], F32, kind="ExternalInput").ap()
    d_wuT = nc.dram_tensor("wuT", [H // 8, H], F32, kind="ExternalInput").ap()
    d_gwT = nc.dram_tensor("gwT", [128, NHK * 4], F32, kind="ExternalInput").ap()
    d_laT = nc.dram_tensor("laT", [128, NHK * 32], F32, kind="ExternalInput").ap()
    d_lbar = nc.dram_tensor("lbar", [32, H], F32, kind="ExternalInput").ap()
    d_eb = nc.dram_tensor("eb", [4, 1], F32, kind="ExternalInput").ap()

    d_out = nc.dram_tensor("out", [1024, H], F32, kind="ExternalOutput").ap()

    d_winT = nc.dram_tensor("winTt", [8 * 32 * 128, 512], BF16).ap()
    d_woTb = nc.dram_tensor("woTb", [DIN, H], BF16).ap()
    d_wuTb = nc.dram_tensor("wuTb", [H, H], BF16).ap()
    d_woTsh = nc.dram_tensor("woTsh", [DIN // 8, H], BF16).ap()
    d_wuTsh = nc.dram_tensor("wuTsh", [H // 8, H], BF16).ap()
    d_smp = [nc.dram_tensor(f"smp{c}", [ESSM, TCH], F32).ap() for c in range(NCH)]
    d_smr = [nc.dram_tensor(f"smr{c}", [ESSM, TCH], F32).ap() for c in range(NCH)]
    d_br = nc.dram_tensor("br", [NCH * NST, TCH], BF16).ap()
    d_cr = nc.dram_tensor("cr", [NCH * NST, TCH], BF16).ap()
    d_yin = nc.dram_tensor("yin", [NCH * DG, TCH], BF16).ap()
    d_yrc = nc.dram_tensor("yrc", [NCH * DG, TCH], BF16).ap()
    d_ohs = nc.dram_tensor("ohs", [4, TCH], BF16).ap()

    ctr = Ctr()

    es = contextlib.ExitStack()
    with es:
        r_A = es.enter_context(nc.sbuf_tensor("r_A", [128, NDT * NST], F32))
        r_cw = es.enter_context(nc.sbuf_tensor("r_cw", [128, NDT * KCV], F32))
        r_cb = es.enter_context(nc.sbuf_tensor("r_cb", [128, NDT], F32))
        r_dtb = es.enter_context(nc.sbuf_tensor("r_dtb", [128, NDT], F32))
        r_dpar = es.enter_context(nc.sbuf_tensor("r_dpar", [128, NDT], F32))
        r_dtlnw = es.enter_context(nc.sbuf_tensor("r_dtlnw", [128, 2], F32))
        r_blnw = es.enter_context(nc.sbuf_tensor("r_blnw", [NST, 1], F32))
        r_clnw = es.enter_context(nc.sbuf_tensor("r_clnw", [NST, 1], F32))
        r_eb = es.enter_context(nc.sbuf_tensor("r_eb", [4, 1], F32))
        r_xpwT = es.enter_context(nc.sbuf_tensor("r_xpwT", [128, NDT, ESSM], BF16))
        r_dtpwT = es.enter_context(nc.sbuf_tensor("r_dtpwT", [128, 2, DG], BF16))
        r_ones_b = es.enter_context(nc.sbuf_tensor("r_ones_b", [128, 1], BF16))
        r_ones16_b = es.enter_context(nc.sbuf_tensor("r_ones16_b", [NST, 1], BF16))
        r_onesr_f = es.enter_context(nc.sbuf_tensor("r_onesr_f", [1, 128], F32))
        r_idf = es.enter_context(nc.sbuf_tensor("r_idf", [128, 128], F32))
        r_eps1 = es.enter_context(nc.sbuf_tensor("r_eps1", [128, 1], F32))
        r_eps2 = es.enter_context(nc.sbuf_tensor("r_eps2", [128, 1], F32))
        ps_g0 = es.enter_context(nc.psum_tensor("ps_g0", [128, TCH], F32))
        ps_g1 = es.enter_context(nc.psum_tensor("ps_g1", [128, TCH], F32))
        ps_g2 = es.enter_context(nc.psum_tensor("ps_g2", [128, TCH], F32))
        ps_g3 = es.enter_context(nc.psum_tensor("ps_g3", [128, TCH], F32))
        ps_x = es.enter_context(nc.psum_tensor("ps_x", [128, TCH], F32))
        ps_s1 = es.enter_context(nc.psum_tensor("ps_s1", [128, TCH], F32))
        ps_s2 = es.enter_context(nc.psum_tensor("ps_s2", [128, TCH], F32))
        ps_bc = es.enter_context(nc.psum_tensor("ps_bc", [128, TCH], F32))
        sQL = es.enter_context(nc.semaphore("sQL"))
        sQG = es.enter_context(nc.semaphore("sQG"))
        sCC = es.enter_context(nc.semaphore("sCC"))
        sPE = es.enter_context(nc.semaphore("sPE"))
        sV = es.enter_context(nc.semaphore("sV"))
        sA = es.enter_context(nc.semaphore("sA"))
        sG = es.enter_context(nc.semaphore("sG"))
        sST = es.enter_context(nc.semaphore("sST"))
        sB = es.enter_context(nc.semaphore("sB"))
        block = es.enter_context(nc.Block())
        nc.const_aps.register(F32, EPS_LN, r_eps1[:])
        nc.const_aps.register(F32, EPS_RMS, r_eps2[:])
        PSG = [ps_g0, ps_g1, ps_g2, ps_g3]
        SEM = dict(sQL=sQL, sQG=sQG, sCC=sCC, sPE=sPE, sV=sV, sA=sA, sG=sG, sST=sST, sB=sB)
        SEC = dict(sync=block.sync, vector=block.vector, scalar=block.scalar,
                   gpsimd=block.gpsimd, tensor=block.tensor)

        def inc(instr, sem, n=1):
            instr.then_inc(SEM[sem], n)
            ctr.add(sem, n)
            return ctr.get(sem)

        def dump(name, ap, sem, val):
            if not DBG:
                return
            shp = list(ap.shape)
            d = nc.dram_tensor("dbg_" + name, shp, ap.dtype, kind="ExternalOutput").ap()
            def f(q, d=d, ap=ap, sem=sem, val=val):
                q.wait_ge(SEM[sem], val)
                q.dma_start(out=d[tuple(slice(None) for _ in shp)], in_=ap).then_inc(sST, 16)
            block.sync(f)
            ctr.add("sST", 16)

        def barrier():
            tgt = ctr.get("sB") + 5
            for e in ENGS:
                def f(q, e=e):
                    q.sem_inc(sB, 1)
                    q.wait_ge(sB, tgt)
                SEC[e](f)
            ctr.add("sB", 5)

        with (
            nc.sbuf_tensor("p0_st", [128, 4608], F32) as p0_st,
            nc.sbuf_tensor("p0_stb", [128, 4608], BF16) as p0_stb,
            nc.sbuf_tensor("p0_idi", [128, 128], I32) as p0_idi,
        ):
            def p0_res(q):
                q.dma_start(out=r_A[:], in_=d_alog[:, :]).then_inc(sQL, 16)
                q.dma_start(out=r_cw[:], in_=d_cw[:, :]).then_inc(sQL, 16)
                q.dma_start(out=r_cb[:], in_=d_cb[:, :]).then_inc(sQL, 16)
                q.dma_start(out=r_dtb[:], in_=d_dtb[:, :]).then_inc(sQL, 16)
                q.dma_start(out=r_dpar[:], in_=d_dpar[:, :]).then_inc(sQL, 16)
                q.dma_start(out=r_dtlnw[:], in_=d_dtlnw[:, :]).then_inc(sQL, 16)
                q.dma_start(out=r_blnw[:], in_=d_blnw[:, :]).then_inc(sQL, 16)
                q.dma_start(out=r_clnw[:], in_=d_clnw[:, :]).then_inc(sQL, 16)
                q.dma_start(out=r_eb[:], in_=d_eb[:, :]).then_inc(sQL, 16)
                q.dma_start(out=p0_st[:, 0:NDT * ESSM], in_=d_xpwT[:, :]).then_inc(sQL, 16)
            block.sync(p0_res)
            t0 = ctr.add("sQL", 160)

            def p0_v1(v):
                v.wait_ge(sQL, t0)
                v.memset(r_eps1[:], EPS_LN)
                v.memset(r_eps2[:], EPS_RMS)
                v.memset(r_ones_b[:], 1.0)
                v.memset(r_ones16_b[:], 1.0)
                v.memset(r_onesr_f[:], 1.0)
                inc(v.tensor_copy(out=r_xpwT[:].rearrange("p a e -> p (a e)"),
                                  in_=p0_st[:, 0:NDT * ESSM]), "sV")
            block.vector(p0_v1)
            tv1 = ctr.get("sV")

            def p0_s2(q):
                q.wait_ge(sV, tv1)
                q.dma_start(out=p0_st[:, 0:2 * DG], in_=d_dtpwT[:, :]).then_inc(sQL, 16)
            block.sync(p0_s2)
            t1 = ctr.add("sQL", 16)

            def p0_v2(v):
                v.wait_ge(sQL, t1)
                inc(v.tensor_copy(out=r_dtpwT[:].rearrange("p a d -> p (a d)"),
                                  in_=p0_st[:, 0:2 * DG]), "sV")
            block.vector(p0_v2)

            def p0_sc(s):
                s.wait_ge(sQL, t0)
                inc(s.activation(out=r_A[:], in_=r_A[:], func=AF.Exp), "sA")
            block.scalar(p0_sc)
            ta = ctr.get("sA")

            def p0_v3(v):
                v.wait_ge(sA, ta)
                inc(v.tensor_scalar(out=r_A[:], in0=r_A[:], scalar1=-1.0, scalar2=None,
                                    op0=ALU.mult), "sV")
            block.vector(p0_v3)

            def p0_g(g):
                g.iota(out=p0_idi[:], pattern=[[1, 128]], base=0, channel_multiplier=-1)
                inc(g.tensor_scalar(out=r_idf[:], in0=p0_idi[:], scalar1=0.0, scalar2=None,
                                    op0=ALU.is_equal), "sG")
            block.gpsimd(p0_g)

            for i in range(NHK):
                def ld(q, i=i):
                    q.wait_ge(sA, ctr.get("sA"))
                    q.wait_ge(sV, ctr.get("sV"))
                    q.dma_start(out=p0_st[:, 0:4096], in_=d_win[128 * i:128 * (i + 1), :]).then_inc(sQL, 16)
                block.sync(ld)
                t = ctr.add("sQL", 16)

                def cv(s, t=t, _tg=None):
                    s.wait_ge(sQL, t)
                    s.wait_ge(sQG, ctr.get("sQG"))
                    inc(s.activation(out=p0_stb[:, 0:4096], in_=p0_st[:, 0:4096], func=AF.Copy), "sA")
                block.scalar(cv)
                tc = ctr.get("sA")

                def st(q, i=i, tc=tc):
                    q.wait_ge(sA, tc)
                    for pog in range(8):
                        base = (pog * 32 + i) * 128
                        q.dma_start(out=d_winT[base:base + 128, :],
                                    in_=p0_stb[:, 512 * pog:512 * (pog + 1)]).then_inc(sQG, 16)
                block.gpsimd(st)
                ctr.add("sQG", 8 * 16)

            T_WINT = ctr.get("sQG")

            for i in range(DIN // 8 // 128):
                def ld(q, i=i):
                    q.wait_ge(sA, ctr.get("sA"))
                    q.dma_start(out=p0_st[:, 0:4096], in_=d_woT[128 * i:128 * (i + 1), :]).then_inc(sQL, 16)
                block.sync(ld)
                t = ctr.add("sQL", 16)
                def cv(s, t=t, _tg=None):
                    s.wait_ge(sQL, t)
                    s.wait_ge(sQG, ctr.get("sQG"))
                    inc(s.activation(out=p0_stb[:, 0:4096], in_=p0_st[:, 0:4096], func=AF.Copy), "sA")
                block.scalar(cv)
                tc = ctr.get("sA")
                def st(q, i=i, tc=tc):
                    q.wait_ge(sA, tc)
                    q.dma_start(out=d_woTsh[128 * i:128 * (i + 1), :], in_=p0_stb[:, 0:4096]).then_inc(sQG, 16)
                block.gpsimd(st)
                ctr.add("sQG", 16)
            for i in range(H // 8 // 128):
                def ld(q, i=i):
                    q.wait_ge(sA, ctr.get("sA"))
                    q.dma_start(out=p0_st[:, 0:4096], in_=d_wuT[128 * i:128 * (i + 1), :]).then_inc(sQL, 16)
                block.sync(ld)
                t = ctr.add("sQL", 16)
                def cv(s, t=t, _tg=None):
                    s.wait_ge(sQL, t)
                    s.wait_ge(sQG, ctr.get("sQG"))
                    inc(s.activation(out=p0_stb[:, 0:4096], in_=p0_st[:, 0:4096], func=AF.Copy), "sA")
                block.scalar(cv)
                tc = ctr.get("sA")
                def st(q, i=i, tc=tc):
                    q.wait_ge(sA, tc)
                    q.dma_start(out=d_wuTsh[128 * i:128 * (i + 1), :], in_=p0_stb[:, 0:4096]).then_inc(sQG, 16)
                block.gpsimd(st)
                ctr.add("sQG", 16)

            T_WSH = ctr.get("sQG")
        T_P0CC = None

        barrier()

        with contextlib.ExitStack() as es1:
            m_xn1 = es1.enter_context(nc.sbuf_tensor("m_xn1", [128, NHK, TCH], BF16))
            m_xc = es1.enter_context(nc.sbuf_tensor("m_xc", [128, NDT, TCH], BF16))
            m_zs = es1.enter_context(nc.sbuf_tensor("m_zs", [128, NDT, TCH], BF16))
            m_brep = es1.enter_context(nc.sbuf_tensor("m_brep", [128, NST * TCH], BF16))
            m_crep = es1.enter_context(nc.sbuf_tensor("m_crep", [128, NST * TCH], BF16))
            m_a = es1.enter_context(nc.sbuf_tensor("m_a", [128, 4 * TCH], F32))
            m_w = es1.enter_context(nc.sbuf_tensor("m_w", [128, 4 * TCH], BF16))
            m_h = es1.enter_context(nc.sbuf_tensor("m_h", [128, 4 * TCH], F32))
            m_hc = es1.enter_context(nc.sbuf_tensor("m_hc", [128, 4 * TCH], BF16))
            m_th = es1.enter_context(nc.sbuf_tensor("m_th", [128, 4, TCH], BF16))
            m_ws = es1.enter_context(nc.sbuf_tensor("m_ws", [128, 2, 8, TCH], BF16))
            m_pad = es1.enter_context(nc.sbuf_tensor("m_pad", [128, KCV - 1 + TCH], F32))
            m_cva = es1.enter_context(nc.sbuf_tensor("m_cva", [128, TCH], F32))
            m_cvb = es1.enter_context(nc.sbuf_tensor("m_cvb", [128, TCH], F32))
            m_smp = es1.enter_context(nc.sbuf_tensor("m_smp", [128, 3, TCH], F32))
            m_dtr = es1.enter_context(nc.sbuf_tensor("m_dtr", [128, 2, TCH], BF16))
            m_bc = es1.enter_context(nc.sbuf_tensor("m_bc", [NST, 2, TCH], F32))
            m_bcn = es1.enter_context(nc.sbuf_tensor("m_bcn", [NST, 2, TCH], BF16))
            m_dtsp = es1.enter_context(nc.sbuf_tensor("m_dtsp", [128, 2, TCH], F32))
            m_dtx = es1.enter_context(nc.sbuf_tensor("m_dtx", [128, TCH], BF16))
            m_sq = es1.enter_context(nc.sbuf_tensor("m_sq", [128, TCH], BF16))
            m_sq2 = es1.enter_context(nc.sbuf_tensor("m_sq2", [128, TCH], BF16))
            m_mu = es1.enter_context(nc.sbuf_tensor("m_mu", [1, TCH], F32))
            m_rsd = es1.enter_context(nc.sbuf_tensor("m_rsd", [1, TCH], F32))
            m_tmp = es1.enter_context(nc.sbuf_tensor("m_tmp", [1, TCH], F32))
            m_carry = es1.enter_context(nc.sbuf_tensor("m_carry", [128, NDT * NST], F32))
            m_halo = es1.enter_context(nc.sbuf_tensor("m_halo", [128, NDT, KCV - 1], F32))
            m_yout = es1.enter_context(nc.sbuf_tensor("m_yout", [128, TCH], BF16))
            xs = m_a[:].rearrange("p (a t) -> p a t", a=4)

            WSR = {'par': 0, 'last': [0, 0]}

            def m_init(v):
                v.memset(m_carry[:], 0.0)
                inc(v.memset(m_halo[:].rearrange("p a k -> p (a k)"), 0.0), "sV")
            block.vector(m_init)

            for ch in range(NCH):
                for grp in range(8):
                    def xl(q, grp=grp, ch=ch):
                        q.wait_ge(sV, ctr.get("sV"))
                        q.wait_ge(sA, ctr.get("sA"))
                        q.dma_start(out=xs[:, :, :],
                                    in_=d_xTb.rearrange("(a p) t -> p a t", p=128)[
                                        :, 4 * grp:4 * (grp + 1), TCH * ch:TCH * (ch + 1)]).then_inc(sQL, 16)
                    block.sync(xl)
                    t_x = ctr.add("sQL", 16)

                    def xc_cast(v, grp=grp, t_x=t_x):
                        v.wait_ge(sQL, t_x)
                        v.wait_ge(sPE, ctr.get("sPE"))
                        inc(v.tensor_copy(out=m_xn1[:, 4 * grp:4 * (grp + 1), :].rearrange("p a t -> p (a t)"),
                                          in_=xs[:, :, :].rearrange("p a t -> p (a t)")), "sV")
                    block.vector(xc_cast)
                    t_c = ctr.get("sV")

                    def sqg(s, grp=grp, t_c=t_c):
                        s.wait_ge(sV, t_c)
                        s.wait_ge(sPE, ctr.get("sPE"))
                        i = None
                        for j in range(4):
                            i = s.activation(out=m_th[:, j, :], in_=xs[:, j, :], func=AF.Square)
                        inc(i, "sA")
                    block.scalar(sqg)
                    t_s = ctr.get("sA")

                    def mmg4(t, grp=grp, t_c=t_c, t_s=t_s, ch=ch):
                        t.wait_ge(sV, t_c)
                        t.wait_ge(sA, t_s)
                        i = None
                        for j in range(4):
                            k = 4 * grp + j
                            t.matmul(ps_s1[0:1, :], r_ones_b[:], m_xn1[:, k, :],
                                     start=(k == 0), stop=(k == NHK - 1))
                            i = t.matmul(ps_s2[0:1, :], r_ones_b[:], m_th[:, j, :],
                                         start=(k == 0), stop=(k == NHK - 1))
                        inc(i, "sPE")
                    block.tensor(mmg4)
                t_st = ctr.get("sPE")

                def ln_f1(v, ch=ch):
                    v.wait_ge(sPE, t_st)
                    v.tensor_scalar(out=m_mu[:], in0=ps_s1[0:1, :], scalar1=1.0 / H,
                                    scalar2=None, op0=ALU.mult)
                    inc(v.tensor_scalar(out=m_tmp[:], in0=ps_s2[0:1, :], scalar1=1.0 / H,
                                        scalar2=None, op0=ALU.mult), "sV")
                block.vector(ln_f1)
                t_f1 = ctr.get("sV")

                def ln_f2(s, ch=ch):
                    s.wait_ge(sV, t_f1)
                    inc(s.activation(out=m_rsd[:], in_=m_mu[:], func=AF.Square), "sA")
                block.scalar(ln_f2)
                t_f2 = ctr.get("sA")

                def ln_f3(v, ch=ch):
                    v.wait_ge(sA, t_f2)
                    v.tensor_tensor(out=m_tmp[:], in0=m_tmp[:], in1=m_rsd[:], op=ALU.subtract)
                    inc(v.tensor_scalar(out=m_mu[:], in0=m_mu[:], scalar1=-1.0, scalar2=None,
                                        op0=ALU.mult), "sV")
                block.vector(ln_f3)
                t_f3 = ctr.get("sV")

                def ln_f4(s, ch=ch):
                    s.wait_ge(sV, t_f3)
                    s.activation(out=m_tmp[:], in_=m_tmp[:], func=AF.Ln, bias=EPS_LN)
                    inc(s.activation(out=m_rsd[:], in_=m_tmp[:], func=AF.Exp, scale=-0.5), "sA")
                block.scalar(ln_f4)
                t_f4 = ctr.get("sA")

                def ln_bc(t, ch=ch):
                    t.wait_ge(sA, t_f4)
                    t.wait_ge(sG, 1)
                    t.matmul(ps_bc[:], r_onesr_f[:], m_mu[:], start=True, stop=True)
                    inc(t.matmul(ps_x[:], r_onesr_f[:], m_rsd[:], start=True, stop=True), "sPE")
                block.tensor(ln_bc)
                t_lbc = ctr.get("sPE")

                for grp in range(8):
                    def xl2(q, grp=grp, ch=ch):
                        q.wait_ge(sV, ctr.get("sV"))
                        q.dma_start(out=xs[:, :, :],
                                    in_=d_xTb.rearrange("(a p) t -> p a t", p=128)[
                                        :, 4 * grp:4 * (grp + 1), TCH * ch:TCH * (ch + 1)]).then_inc(sQL, 16)
                    block.sync(xl2)
                    t_x2 = ctr.add("sQL", 16)

                    def nrm(v, grp=grp, t_x2=t_x2, ch=ch):
                        v.wait_ge(sQL, t_x2)
                        v.wait_ge(sPE, t_lbc)
                        v.tensor_tensor(out=xs[:, :, :], in0=xs[:, :, :],
                                        in1=ps_bc[:].unsqueeze(1).broadcast_to((128, 4, TCH)), op=ALU.add)
                        inc(v.tensor_tensor(out=m_xn1[:, 4 * grp:4 * (grp + 1), :],
                                            in0=xs[:, :, :],
                                            in1=ps_x[:].unsqueeze(1).broadcast_to((128, 4, TCH)),
                                            op=ALU.mult), "sV")
                    block.vector(nrm)
                t_xn1 = ctr.get("sV")
                if ch == 0:
                    dump("xn1", m_xn1[:], "sV", t_xn1)
                    dump("mu", m_mu[:], "sV", t_xn1)
                    dump("rsd", m_rsd[:], "sV", t_xn1)

                for pog in range(4):
                    for kq in range(4):
                        par = WSR['par']
                        def wld(q, pog=pog, kq=kq, par=par, ch=ch,
                                t_free=WSR['last'][WSR['par']]):
                            q.wait_ge(sPE, t_free)
                            q.wait_ge(sQG, T_WINT)
                            base = (pog * 32 + 8 * kq) * 128
                            q.dma_start(out=m_ws[:, par, :, :],
                                        in_=d_winT[base:base + 8 * 128, :].rearrange("(a p) c -> p a c", p=128)).then_inc(sQL, 16)
                        block.sync(wld)
                        t_w = ctr.add("sQL", 16)

                        def mmg(t, pog=pog, kq=kq, par=par, t_w=t_w, ch=ch):
                            t.wait_ge(sQL, t_w)
                            t.wait_ge(sV, t_xn1)
                            if kq == 0:
                                t.wait_ge(sA, ctr.get("sA"))
                                t.wait_ge(sV, ctr.get("sV"))
                            i = None
                            for k in range(8):
                                kk = 8 * kq + k
                                for poi in range(4):
                                    i = t.matmul(PSG[poi][:],
                                                 m_ws[:, par, k, :].rearrange("p (x c) -> p x c", x=4)[:, poi, :],
                                                 m_xn1[:, kk, :], start=(kk == 0), stop=(kk == 31))
                            inc(i, "sPE")
                        block.tensor(mmg)
                        WSR['last'][par] = ctr.get("sPE")
                        WSR['par'] = 1 - par
                    t_mm = ctr.get("sPE")

                    for poi in range(4):
                        dtile = 4 * (pog % 4) + poi
                        if True:
                            def cpad(s, poi=poi, t_mm=t_mm, ch=ch):
                                s.wait_ge(sPE, t_mm)
                                s.wait_ge(sV, ctr.get("sV"))
                                inc(s.copy(out=m_pad[:, KCV - 1:], in_=PSG[poi][:]), "sA")
                            block.scalar(cpad)
                            t_p = ctr.get("sA")

                            def ctaps(v, dtile=dtile, t_p=t_p, ch=ch):
                                v.wait_ge(sA, t_p)
                                v.tensor_copy(out=m_pad[:, 0:KCV - 1], in_=m_halo[:, dtile, :])
                                v.tensor_copy(out=m_halo[:, dtile, :], in_=m_pad[:, TCH:TCH + KCV - 1])
                                v.tensor_scalar(out=m_cva[:], in0=m_pad[:, 0:TCH],
                                                scalar1=r_cw[:, KCV * dtile:KCV * dtile + 1],
                                                scalar2=r_cb[:, dtile:dtile + 1],
                                                op0=ALU.mult, op1=ALU.add)
                                v.scalar_tensor_tensor(out=m_cvb[:], in0=m_pad[:, 1:1 + TCH],
                                                       scalar=r_cw[:, KCV * dtile + 1:KCV * dtile + 2],
                                                       in1=m_cva[:], op0=ALU.mult, op1=ALU.add)
                                v.scalar_tensor_tensor(out=m_cva[:], in0=m_pad[:, 2:2 + TCH],
                                                       scalar=r_cw[:, KCV * dtile + 2:KCV * dtile + 3],
                                                       in1=m_cvb[:], op0=ALU.mult, op1=ALU.add)
                                inc(v.scalar_tensor_tensor(out=m_cvb[:], in0=m_pad[:, 3:3 + TCH],
                                                           scalar=r_cw[:, KCV * dtile + 3:KCV * dtile + 4],
                                                           in1=m_cva[:], op0=ALU.mult, op1=ALU.add), "sV")
                            block.vector(ctaps)
                            t_t = ctr.get("sV")

                            def csilu(s, dtile=dtile, t_t=t_t, ch=ch):
                                s.wait_ge(sV, t_t)
                                inc(s.activation(out=m_xc[:, dtile, :], in_=m_cvb[:], func=AF.Silu), "sA")
                            block.scalar(csilu)
                t_inproj = ctr.get("sA")
                if ch == 0:
                    dump("xc", m_xc[:], "sA", t_inproj)

                for epo in range(3):
                    rows = 128 if epo < 2 else 32
                    def xpm(t, epo=epo, rows=rows, ch=ch):
                        t.wait_ge(sA, t_inproj)
                        if epo > 0:
                            t.wait_ge(sV, ctr.get("sV"))
                        i = None
                        for dtile in range(NDT):
                            i = t.matmul(ps_x[0:rows, :],
                                         r_xpwT[:, dtile, 128 * epo:128 * epo + rows],
                                         m_xc[:, dtile, :], start=(dtile == 0), stop=(dtile == NDT - 1))
                        inc(i, "sPE")
                    block.tensor(xpm)
                    t_xp = ctr.get("sPE")

                    def xpe(v, epo=epo, rows=rows, t_xp=t_xp, ch=ch):
                        v.wait_ge(sPE, t_xp)
                        inc(v.tensor_copy(out=m_smp[0:rows, epo, :], in_=ps_x[0:rows, :]), "sV")
                    block.vector(xpe)
                t_xpe = ctr.get("sV")

                def sst(q, ch=ch):
                    q.wait_ge(sV, t_xpe)
                    q.dma_start(out=d_smp[ch][0:128, :], in_=m_smp[:, 0, :]).then_inc(sQL, 16)
                    q.dma_start(out=d_smp[ch][128:256, :], in_=m_smp[:, 1, :]).then_inc(sQL, 16)
                    q.dma_start(out=d_smp[ch][256:288, :], in_=m_smp[0:32, 2, :]).then_inc(sQL, 16)
                block.sync(sst)
                t_sst = ctr.add("sQL", 48)

                def scc(g, ch=ch):
                    g.wait_ge(sQL, t_sst)
                    g.collective_compute("AllReduce", ALU.add, replica_groups=G4,
                                         ins=[d_smp[ch].opt()], outs=[d_smr[ch].opt()]).then_inc(sCC)
                block.gpsimd(scc)
                t_scc = ctr.add("sCC", 1)

                for pog in range(4, 8):
                    for kq in range(4):
                        par = WSR['par']
                        def wldz(q, pog=pog, kq=kq, par=par, ch=ch,
                                 t_free=WSR['last'][WSR['par']]):
                            q.wait_ge(sPE, t_free)
                            q.wait_ge(sQG, T_WINT)
                            base = (pog * 32 + 8 * kq) * 128
                            q.dma_start(out=m_ws[:, par, :, :],
                                        in_=d_winT[base:base + 8 * 128, :].rearrange("(a p) c -> p a c", p=128)).then_inc(sQL, 16)
                        block.sync(wldz)
                        t_w = ctr.add("sQL", 16)

                        def mmgz(t, pog=pog, kq=kq, par=par, t_w=t_w, ch=ch):
                            t.wait_ge(sQL, t_w)
                            if kq == 0:
                                t.wait_ge(sA, ctr.get("sA"))
                                t.wait_ge(sV, ctr.get("sV"))
                            i = None
                            for k in range(8):
                                kk = 8 * kq + k
                                for poi in range(4):
                                    i = t.matmul(PSG[poi][:],
                                                 m_ws[:, par, k, :].rearrange("p (x c) -> p x c", x=4)[:, poi, :],
                                                 m_xn1[:, kk, :], start=(kk == 0), stop=(kk == 31))
                            inc(i, "sPE")
                        block.tensor(mmgz)
                        WSR['last'][par] = ctr.get("sPE")
                        WSR['par'] = 1 - par
                    t_mmz = ctr.get("sPE")
                    for poi in range(4):
                        dtile = 4 * (pog % 4) + poi
                        def zsilu(s, dtile=dtile, poi=poi, t_mmz=t_mmz, ch=ch):
                            s.wait_ge(sPE, t_mmz)
                            inc(s.activation(out=m_zs[:, dtile, :], in_=PSG[poi][:], func=AF.Silu), "sA")
                        block.scalar(zsilu)
                if ch == 0:
                    dump("zs", m_zs[:], "sA", ctr.get("sA"))

                def sld(q, ch=ch):
                    q.wait_ge(sCC, t_scc)
                    q.wait_ge(sV, ctr.get("sV"))
                    q.dma_start(out=m_smp[:, 0, :], in_=d_smr[ch][0:128, :]).then_inc(sQL, 16)
                    q.dma_start(out=m_smp[:, 1, :], in_=d_smr[ch][128:256, :]).then_inc(sQL, 16)
                    q.dma_start(out=m_bc[:, 0, :], in_=d_smr[ch][256:272, :]).then_inc(sQL, 16)
                    q.dma_start(out=m_bc[:, 1, :], in_=d_smr[ch][272:288, :]).then_inc(sQL, 16)
                block.sync(sld)
                t_sld = ctr.add("sQL", 64)

                def rsq(s, ch=ch):
                    s.wait_ge(sQL, t_sld)
                    s.wait_ge(sPE, ctr.get("sPE"))
                    s.activation(out=m_sq[:], in_=m_smp[:, 0, :], func=AF.Square)
                    inc(s.activation(out=m_sq2[:], in_=m_smp[:, 1, :], func=AF.Square), "sA")
                block.scalar(rsq)
                t_r1 = ctr.get("sA")

                def rmm(t, ch=ch):
                    t.wait_ge(sA, t_r1)
                    t.matmul(ps_s1[0:1, :], r_ones_b[:], m_sq[:], start=True, stop=False)
                    inc(t.matmul(ps_s1[0:1, :], r_ones_b[:], m_sq2[:], start=False, stop=True), "sPE")
                block.tensor(rmm)
                t_r2 = ctr.get("sPE")

                def rsqbc(s, ch=ch):
                    s.wait_ge(sQL, t_sld)
                    s.activation(out=m_sq[0:NST, :], in_=m_bc[:, 0, :], func=AF.Square)
                    inc(s.activation(out=m_sq2[0:NST, :], in_=m_bc[:, 1, :], func=AF.Square), "sA")
                block.scalar(rsqbc)
                t_r3 = ctr.get("sA")

                def rmmbc(t, ch=ch):
                    t.wait_ge(sA, t_r3)
                    t.matmul(ps_s2[0:1, :], r_ones16_b[:], m_sq[0:NST, :], start=True, stop=True)
                    inc(t.matmul(ps_bc[0:1, :], r_ones16_b[:], m_sq2[0:NST, :], start=True, stop=True), "sPE")
                block.tensor(rmmbc)
                t_r4 = ctr.get("sPE")

                def rrsd(s, ch=ch):
                    s.wait_ge(sPE, max(t_r2, t_r4))
                    s.activation(out=m_mu[:], in_=ps_s1[0:1, :], func=AF.Ln, scale=1.0 / RDT, bias=EPS_RMS)
                    s.activation(out=m_mu[:], in_=m_mu[:], func=AF.Exp, scale=-0.5)
                    s.activation(out=m_rsd[:], in_=ps_s2[0:1, :], func=AF.Ln, scale=1.0 / NST, bias=EPS_RMS)
                    s.activation(out=m_rsd[:], in_=m_rsd[:], func=AF.Exp, scale=-0.5)
                    s.activation(out=m_tmp[:], in_=ps_bc[0:1, :], func=AF.Ln, scale=1.0 / NST, bias=EPS_RMS)
                    inc(s.activation(out=m_tmp[:], in_=m_tmp[:], func=AF.Exp, scale=-0.5), "sA")
                block.scalar(rrsd)
                t_r5 = ctr.get("sA")

                def rbc(t, ch=ch):
                    t.wait_ge(sA, t_r5)
                    t.matmul(ps_bc[:], r_onesr_f[:], m_mu[:], start=True, stop=True)
                    t.matmul(ps_s1[0:NST, :], r_onesr_f[0:1, 0:NST], m_rsd[:], start=True, stop=True)
                    inc(t.matmul(ps_s2[0:NST, :], r_onesr_f[0:1, 0:NST], m_tmp[:], start=True, stop=True), "sPE")
                block.tensor(rbc)
                t_r6 = ctr.get("sPE")

                def rap(v, ch=ch):
                    v.wait_ge(sPE, t_r6)
                    v.scalar_tensor_tensor(out=m_dtr[:, 0, :], in0=m_smp[:, 0, :], scalar=r_dtlnw[:, 0:1],
                                           in1=ps_bc[:], op0=ALU.mult, op1=ALU.mult)
                    v.scalar_tensor_tensor(out=m_dtr[:, 1, :], in0=m_smp[:, 1, :], scalar=r_dtlnw[:, 1:2],
                                           in1=ps_bc[:], op0=ALU.mult, op1=ALU.mult)
                    v.scalar_tensor_tensor(out=m_bcn[:, 0, :], in0=m_bc[:, 0, :], scalar=r_blnw[:, 0:1],
                                           in1=ps_s1[0:NST, :], op0=ALU.mult, op1=ALU.mult)
                    inc(v.scalar_tensor_tensor(out=m_bcn[:, 1, :], in0=m_bc[:, 1, :], scalar=r_clnw[:, 0:1],
                                               in1=ps_s2[0:NST, :], op0=ALU.mult, op1=ALU.mult), "sV")
                block.vector(rap)
                t_r7 = ctr.get("sV")
                if ch == 0:
                    dump("smp", m_smp[:], "sV", t_r7)
                    dump("dtr", m_dtr[:], "sV", t_r7)
                    dump("bcn", m_bcn[:], "sV", t_r7)

                def bst(q, ch=ch):
                    q.wait_ge(sV, t_r7)
                    q.dma_start(out=d_br[NST * ch:NST * (ch + 1), :], in_=m_bcn[:, 0, :]).then_inc(sQL, 16)
                    q.dma_start(out=d_cr[NST * ch:NST * (ch + 1), :], in_=m_bcn[:, 1, :]).then_inc(sQL, 16)
                block.sync(bst)
                t_bst = ctr.add("sQL", 32)

                def brl(q, ch=ch):
                    q.wait_ge(sQL, t_bst)
                    q.wait_ge(sV, ctr.get("sV"))
                    q.wait_ge(sG, ctr.get("sG"))
                    q.dma_start(out=m_brep[:],
                                in_=d_br[NST * ch:NST * (ch + 1), :].rearrange("n t -> (n t)").unsqueeze(0).broadcast_to((128, NST * TCH))).then_inc(sQL, 16)
                    q.dma_start(out=m_crep[:],
                                in_=d_cr[NST * ch:NST * (ch + 1), :].rearrange("n t -> (n t)").unsqueeze(0).broadcast_to((128, NST * TCH))).then_inc(sQL, 16)
                block.sync(brl)
                t_brl = ctr.add("sQL", 32)

                for dtile in range(NDT):
                    dts = m_dtsp[:, dtile % 2, :]
                    def dpm(t, dtile=dtile, ch=ch):
                        t.wait_ge(sV, t_r7)
                        t.wait_ge(sA, ctr.get("sA"))
                        t.matmul(ps_x[:], r_dtpwT[:, 0, 128 * dtile:128 * (dtile + 1)], m_dtr[:, 0, :],
                                 start=True, stop=False)
                        inc(t.matmul(ps_x[:], r_dtpwT[:, 1, 128 * dtile:128 * (dtile + 1)], m_dtr[:, 1, :],
                                     start=False, stop=True), "sPE")
                    block.tensor(dpm)
                    t_d1 = ctr.get("sPE")

                    def sp0(v, dtile=dtile, t_d1=t_d1, ch=ch):
                        v.wait_ge(sPE, t_d1)
                        v.wait_ge(sA, ctr.get("sA"))
                        v.tensor_scalar(out=m_cvb[:], in0=ps_x[:], scalar1=r_dtb[:, dtile:dtile + 1],
                                        scalar2=None, op0=ALU.add)
                        v.tensor_scalar(out=m_cva[:], in0=m_cvb[:], scalar1=-1.0, scalar2=None, op0=ALU.mult)
                        v.tensor_tensor(out=m_cva[:], in0=m_cva[:], in1=m_cvb[:], op=ALU.max)
                        inc(v.tensor_scalar(out=m_cvb[:], in0=m_cvb[:], scalar1=0.0,
                                            scalar2=None, op0=ALU.max), "sV")
                    block.vector(sp0)
                    t_d1v = ctr.get("sV")

                    def sp(s, dtile=dtile, dts=dts, t_d1v=t_d1v, ch=ch):
                        s.wait_ge(sV, t_d1v)
                        s.activation(out=m_cva[:], in_=m_cva[:], func=AF.Exp, scale=-1.0)
                        inc(s.activation(out=m_cva[:], in_=m_cva[:], func=AF.Ln, bias=1.0), "sA")
                    block.scalar(sp)
                    t_d2 = ctr.get("sA")

                    def spa(v, dts=dts, dtile=dtile, t_d2=t_d2, ch=ch):
                        v.wait_ge(sA, t_d2)
                        v.tensor_tensor(out=dts, in0=m_cva[:], in1=m_cvb[:], op=ALU.add)
                        inc(v.tensor_tensor(out=m_dtx[:], in0=dts, in1=m_xc[:, dtile, :], op=ALU.mult), "sV")
                    block.vector(spa)
                    t_d3 = ctr.get("sV")
                    if ch == 0 and dtile == 0:
                        dump("dtsp", dts, "sV", t_d3)
                        dump("dtx", m_dtx[:], "sV", t_d3)

                    for quar in range(4):
                        nb = 4 * quar
                        def aexp(s, dtile=dtile, nb=nb, dts=dts, t_d3=t_d3, ch=ch):
                            s.wait_ge(sV, t_d3)
                            s.wait_ge(sV, ctr.get("sV"))
                            i = None
                            for j in range(4):
                                n = NST * dtile + nb + j
                                i = s.activation(out=m_a[:, TCH * j:TCH * (j + 1)], in_=dts,
                                                 func=AF.Exp, scale=r_A[:, n:n + 1])
                            inc(i, "sA")
                        block.scalar(aexp)
                        t_h1 = ctr.get("sA")

                        def wm(v, nb=nb, t_h1=t_h1, ch=ch, dtile=dtile):
                            v.wait_ge(sQL, t_brl)
                            v.wait_ge(sV, t_d3)
                            i = None
                            for j in range(4):
                                i = v.tensor_tensor(out=m_w[:, TCH * j:TCH * (j + 1)], in0=m_dtx[:],
                                                    in1=m_brep[:, TCH * (nb + j):TCH * (nb + j + 1)],
                                                    op=ALU.mult)
                            inc(i, "sV")
                        block.vector(wm)
                        t_h2 = ctr.get("sV")

                        def scans(v, dtile=dtile, nb=nb, t_h1=t_h1, ch=ch):
                            v.wait_ge(sA, t_h1)
                            v.wait_ge(sG, ctr.get("sG"))
                            i = None
                            for j in range(4):
                                n = NST * dtile + nb + j
                                i = v.tensor_tensor_scan(out=m_h[:, TCH * j:TCH * (j + 1)],
                                                         data0=m_a[:, TCH * j:TCH * (j + 1)],
                                                         data1=m_w[:, TCH * j:TCH * (j + 1)],
                                                         initial=m_carry[:, n:n + 1],
                                                         op0=ALU.mult, op1=ALU.add)
                            inc(i, "sV")
                        block.vector(scans)
                        t_h3 = ctr.get("sV")
                        if ch == 0 and dtile == 0 and quar == 0:
                            dump("sa", m_a[:], "sV", t_h3)
                            dump("sw", m_w[:], "sV", t_h3)
                            dump("sh", m_h[:], "sV", t_h3)

                        def cex(s, dtile=dtile, nb=nb, t_h3=t_h3, ch=ch):
                            s.wait_ge(sV, t_h3)
                            n0 = NST * dtile + nb
                            inc(s.copy(out=m_carry[:, n0:n0 + 4], in_=m_h[:, TCH - 1::TCH]), "sA")
                        block.scalar(cex)

                        def hc(g, nb=nb, t_h3=t_h3, ch=ch, dtile=dtile):
                            g.wait_ge(sV, t_h3)
                            inc(g.tensor_tensor(out=m_hc[:], in0=m_h[:],
                                                in1=m_crep[:, TCH * nb:TCH * (nb + 4)], op=ALU.mult), "sG")
                        block.gpsimd(hc)
                        t_h4 = ctr.get("sG")

                        def tre(v, quar=quar, t_h4=t_h4, ch=ch, dtile=dtile):
                            v.wait_ge(sG, t_h4)
                            v.tensor_tensor(out=m_hc[:, 0:2 * TCH], in0=m_hc[:, 0:2 * TCH],
                                            in1=m_hc[:, 2 * TCH:4 * TCH], op=ALU.add)
                            inc(v.tensor_tensor(out=m_th[:, quar, :], in0=m_hc[:, 0:TCH],
                                                in1=m_hc[:, TCH:2 * TCH], op=ALU.add), "sV")
                        block.vector(tre)

                    def yasm(v, dtile=dtile, ch=ch):
                        v.wait_ge(sV, ctr.get("sV"))
                        v.tensor_tensor(out=m_cva[:], in0=m_th[:, 0, :], in1=m_th[:, 1, :], op=ALU.add)
                        v.tensor_tensor(out=m_cvb[:], in0=m_th[:, 2, :], in1=m_th[:, 3, :], op=ALU.add)
                        v.tensor_tensor(out=m_cva[:], in0=m_cva[:], in1=m_cvb[:], op=ALU.add)
                        v.scalar_tensor_tensor(out=m_cvb[:], in0=m_xc[:, dtile, :],
                                               scalar=r_dpar[:, dtile:dtile + 1],
                                               in1=m_cva[:], op0=ALU.mult, op1=ALU.add)
                        inc(v.tensor_tensor(out=m_yout[:], in0=m_cvb[:], in1=m_zs[:, dtile, :],
                                            op=ALU.mult), "sV")
                    block.vector(yasm)
                    t_y = ctr.get("sV")

                    if ch == 0 and dtile == 0:
                        dump("yout", m_yout[:], "sV", t_y)
                    def yst(q, dtile=dtile, ch=ch, t_y=t_y):
                        q.wait_ge(sV, t_y)
                        r0 = DG * ch + 128 * dtile
                        q.dma_start(out=d_yin[r0:r0 + 128, :], in_=m_yout[:]).then_inc(sQG, 16)
                    block.gpsimd(yst)
                    ctr.add("sQG", 16)

            def wcc(g):
                g.wait_ge(sQG, T_WSH)
                g.collective_compute("AllGather", ALU.bypass, replica_groups=G8,
                                     ins=[d_woTsh.opt()], outs=[d_woTb.opt()]).then_inc(sCC)
                g.collective_compute("AllGather", ALU.bypass, replica_groups=G8,
                                     ins=[d_wuTsh.opt()], outs=[d_wuTb.opt()]).then_inc(sCC)
            block.gpsimd(wcc)
            ctr.add("sCC", 2)
            T_P0CC = ctr.get("sCC")

            t_yall = ctr.get("sQG")

            def ycc(g):
                g.wait_ge(sQG, t_yall)
                with allow_ici_all_to_all():
                    g.collective_compute("AllToAll", ALU.bypass, replica_groups=G8,
                                         ins=[d_yin.opt()], outs=[d_yrc.opt()]).then_inc(sCC)
            block.gpsimd(ycc)
            ctr.add("sCC", 1)
        T_P1CC = ctr.get("sCC")

        barrier()

        with contextlib.ExitStack() as es2:
            f_x2 = es2.enter_context(nc.sbuf_tensor("f_x2", [128, NHK, TCH], F32))
            f_big = es2.enter_context(nc.sbuf_tensor("f_big", [128, NDK, TCH], BF16))
            f_ws = es2.enter_context(nc.sbuf_tensor("f_ws", [128, 2, 8, TCH], BF16))
            f_lbar = es2.enter_context(nc.sbuf_tensor("f_lbar", [32, H], BF16))
            f_laT = es2.enter_context(nc.sbuf_tensor("f_laT", [128, NHK, 32], BF16))
            f_gwT = es2.enter_context(nc.sbuf_tensor("f_gwT", [128, NHK, 4], BF16))
            f_sq = es2.enter_context(nc.sbuf_tensor("f_sq", [128, TCH], BF16))
            f_mu = es2.enter_context(nc.sbuf_tensor("f_mu", [1, TCH], F32))
            f_rsd = es2.enter_context(nc.sbuf_tensor("f_rsd", [1, TCH], F32))
            f_tmp = es2.enter_context(nc.sbuf_tensor("f_tmp", [1, TCH], F32))
            f_lg = es2.enter_context(nc.sbuf_tensor("f_lg", [4, TCH], F32))
            f_r1 = es2.enter_context(nc.sbuf_tensor("f_r1", [1, TCH], F32))
            f_r2 = es2.enter_context(nc.sbuf_tensor("f_r2", [1, TCH], F32))
            f_sqB = es2.enter_context(nc.sbuf_tensor("f_sqB", [128, TCH], BF16))
            f_oh0 = es2.enter_context(nc.sbuf_tensor("f_oh0", [1, TCH], BF16))
            f_oh1 = es2.enter_context(nc.sbuf_tensor("f_oh1", [1, TCH], BF16))
            f_oh2 = es2.enter_context(nc.sbuf_tensor("f_oh2", [1, TCH], BF16))
            f_oh3 = es2.enter_context(nc.sbuf_tensor("f_oh3", [1, TCH], BF16))
            f_ohr = es2.enter_context(nc.sbuf_tensor("f_ohr", [32, TCH], BF16))
            f_zg = es2.enter_context(nc.sbuf_tensor("f_zg", [32, TCH], BF16))
            f_of = es2.enter_context(nc.sbuf_tensor("f_of", [128, 4, TCH], F32))
            f_ost4 = es2.enter_context(nc.sbuf_tensor("f_ost4", [128, 2, TCH], F32))
            xn2 = f_big[:]
            FWR = {'par': 0, 'last': [0, 0]}
            f_stg = f_of[:].rearrange("p a t -> p (a t)")

            def f_res_ld(q):
                q.dma_start(out=f_stg[:, 0:NHK * 4], in_=d_gwT[:, :]).then_inc(sQL, 16)
                q.dma_start(out=f_stg[:, 128:128 + NHK * 32], in_=d_laT[:, :]).then_inc(sQL, 16)
                q.dma_start(out=f_stg[0:32, 1280:1280 + 768], in_=d_lbar[:, 0:768]).then_inc(sQL, 16)
            block.sync(f_res_ld)
            tfr = ctr.add("sQL", 48)

            def f_res_cv(v):
                v.wait_ge(sQL, tfr)
                v.tensor_copy(out=f_gwT[:].rearrange("p a e -> p (a e)"), in_=f_stg[:, 0:NHK * 4])
                v.tensor_copy(out=f_laT[:].rearrange("p a e -> p (a e)"), in_=f_stg[:, 128:128 + NHK * 32])
                inc(v.tensor_copy(out=f_lbar[:, 0:768], in_=f_stg[0:32, 1280:1280 + 768]), "sV")
            block.vector(f_res_cv)
            for off in range(768, H, 768):
                w = min(768, H - off)
                def ld(q, off=off, w=w):
                    q.wait_ge(sV, ctr.get("sV"))
                    q.dma_start(out=f_stg[0:32, 0:w], in_=d_lbar[:, off:off + w]).then_inc(sQL, 16)
                block.sync(ld)
                t = ctr.add("sQL", 16)
                def cv(v, off=off, w=w, t=t):
                    v.wait_ge(sQL, t)
                    inc(v.tensor_copy(out=f_lbar[:, off:off + w], in_=f_stg[0:32, 0:w]), "sV")
                block.vector(cv)

            for t2 in range(2):
                def yld(q, t2=t2):
                    q.wait_ge(sCC, T_P1CC)
                    q.wait_ge(sPE, ctr.get("sPE"))
                    for hh in range(2):
                        q.dma_start(out=f_big[:, 32 * hh:32 * (hh + 1), :],
                                    in_=d_yrc[DIN * t2:DIN * (t2 + 1), :].rearrange("(a p) t -> p a t", p=128)[
                                        :, 32 * hh:32 * (hh + 1), :]).then_inc(sQL, 16)
                block.sync(yld)
                t_yl = ctr.add("sQL", 32)

                def xld(q, t2=t2):
                    q.wait_ge(sV, ctr.get("sV"))
                    for hh in range(4):
                        q.dma_start(out=f_x2[:, 8 * hh:8 * (hh + 1), :],
                                    in_=d_xTs.rearrange("(a p) t -> p a t", p=128)[
                                        :, 8 * hh:8 * (hh + 1), 512 * t2:512 * (t2 + 1)]).then_inc(sQL, 16)
                block.sync(xld)
                t_xl = ctr.add("sQL", 64)

                for pog in range(8):
                    for kq in range(8):
                        par = FWR['par']
                        def wol(q, pog=pog, kq=kq, par=par, t2=t2,
                                t_free=FWR['last'][FWR['par']]):
                            q.wait_ge(sCC, T_P0CC)
                            q.wait_ge(sPE, t_free)
                            q.dma_start(out=f_ws[:, par, :, :],
                                        in_=d_woTb.rearrange("(a p) h -> p a h", p=128)[
                                            :, 8 * kq:8 * (kq + 1),
                                            512 * pog:512 * (pog + 1)]).then_inc(sQL, 16)
                        block.sync(wol)
                        t_wo = ctr.add("sQL", 16)

                        def omm(t, pog=pog, kq=kq, par=par, t_wo=t_wo, t2=t2):
                            t.wait_ge(sQL, max(t_wo, t_yl))
                            if kq == 0:
                                t.wait_ge(sV, ctr.get("sV"))
                            i = None
                            for k in range(8):
                                kk = 8 * kq + k
                                for poi in range(4):
                                    i = t.matmul(PSG[poi][:],
                                                 f_ws[:, par, k, :].rearrange("p (x c) -> p x c", x=4)[:, poi, :],
                                                 f_big[:, kk, :],
                                                 start=(kk == 0), stop=(kk == 63))
                            inc(i, "sPE")
                        block.tensor(omm)
                        FWR['last'][par] = ctr.get("sPE")
                        FWR['par'] = 1 - par
                    t_om = ctr.get("sPE")

                    def oev(v, pog=pog, t_om=t_om, t2=t2):
                        v.wait_ge(sPE, t_om)
                        v.wait_ge(sQL, t_xl)
                        i = None
                        for poi in range(4):
                            po = 4 * pog + poi
                            i = v.tensor_tensor(out=f_x2[:, po, :], in0=PSG[poi][:],
                                                in1=f_x2[:, po, :], op=ALU.add)
                        inc(i, "sV")
                    block.vector(oev)
                t_x2d = ctr.get("sV")
                if t2 == 0:
                    dump("yrc", d_yrc[0:2048, :], "sCC", T_P1CC)
                    dump("x2", f_x2[:], "sV", t_x2d)

                def l2call(v, t2=t2):
                    v.wait_ge(sV, t_x2d)
                    v.wait_ge(sPE, ctr.get("sPE"))
                    inc(v.tensor_copy(out=xn2[:, 0:NHK, :].rearrange("p a t -> p (a t)"),
                                      in_=f_x2[:].rearrange("p a t -> p (a t)")), "sV")
                block.vector(l2call)
                t_c2 = ctr.get("sV")
                sqpar = {'v': 0, 'last': [0, 0]}
                for k in range(NHK):
                    par = sqpar['v']
                    sqt = f_sq if par == 0 else f_sqB
                    def l2s(s, k=k, sqt=sqt, t2=t2, t_free=sqpar['last'][par]):
                        if k == 0:
                            s.wait_ge(sV, t_c2)
                        s.wait_ge(sPE, t_free)
                        inc(s.activation(out=sqt[:], in_=f_x2[:, k, :], func=AF.Square), "sA")
                    block.scalar(l2s)
                    t_s2 = ctr.get("sA")
                    def l2m(t, k=k, sqt=sqt, t_s2=t_s2, t2=t2):
                        if k == 0:
                            t.wait_ge(sV, t_c2)
                        t.matmul(ps_s1[0:1, :], r_ones_b[:], xn2[:, k, :],
                                 start=(k == 0), stop=(k == NHK - 1))
                        t.wait_ge(sA, t_s2)
                        i = t.matmul(ps_s2[0:1, :], r_ones_b[:], sqt[:],
                                     start=(k == 0), stop=(k == NHK - 1))
                        inc(i, "sPE")
                    block.tensor(l2m)
                    sqpar['last'][par] = ctr.get("sPE")
                    sqpar['v'] = 1 - par
                t_st2 = ctr.get("sPE")

                def l2f1(v, t2=t2):
                    v.wait_ge(sPE, t_st2)
                    v.tensor_scalar(out=f_mu[:], in0=ps_s1[0:1, :], scalar1=1.0 / H, scalar2=None, op0=ALU.mult)
                    inc(v.tensor_scalar(out=f_tmp[:], in0=ps_s2[0:1, :], scalar1=1.0 / H, scalar2=None, op0=ALU.mult), "sV")
                block.vector(l2f1)
                t_g1 = ctr.get("sV")

                def l2f2(s, t2=t2):
                    s.wait_ge(sV, t_g1)
                    inc(s.activation(out=f_rsd[:], in_=f_mu[:], func=AF.Square), "sA")
                block.scalar(l2f2)
                t_g2 = ctr.get("sA")

                def l2f3(v, t2=t2):
                    v.wait_ge(sA, t_g2)
                    v.tensor_tensor(out=f_tmp[:], in0=f_tmp[:], in1=f_rsd[:], op=ALU.subtract)
                    inc(v.tensor_scalar(out=f_mu[:], in0=f_mu[:], scalar1=-1.0, scalar2=None, op0=ALU.mult), "sV")
                block.vector(l2f3)
                t_g3 = ctr.get("sV")

                def l2f4(s, t2=t2):
                    s.wait_ge(sV, t_g3)
                    s.activation(out=f_tmp[:], in_=f_tmp[:], func=AF.Ln, bias=EPS_LN)
                    inc(s.activation(out=f_rsd[:], in_=f_tmp[:], func=AF.Exp, scale=-0.5), "sA")
                block.scalar(l2f4)
                t_g4 = ctr.get("sA")

                def l2bc(t, t2=t2):
                    t.wait_ge(sA, t_g4)
                    t.matmul(ps_bc[:], r_onesr_f[:], f_mu[:], start=True, stop=True)
                    inc(t.matmul(ps_x[:], r_onesr_f[:], f_rsd[:], start=True, stop=True), "sPE")
                block.tensor(l2bc)
                t_g5 = ctr.get("sPE")

                def l2n(v, t2=t2):
                    v.wait_ge(sPE, t_g5)
                    for hh in range(4):
                        sl = slice(8 * hh, 8 * (hh + 1))
                        v.tensor_tensor(out=f_x2[:, sl, :], in0=f_x2[:, sl, :],
                                        in1=ps_bc[:].unsqueeze(1).broadcast_to((128, 8, TCH)), op=ALU.add)
                        v.tensor_tensor(out=xn2[:, sl, :], in0=f_x2[:, sl, :],
                                        in1=ps_x[:].unsqueeze(1).broadcast_to((128, 8, TCH)), op=ALU.mult)
                    i = None
                    for hh in range(4):
                        sl = slice(8 * hh, 8 * (hh + 1))
                        i = v.tensor_tensor(out=f_x2[:, sl, :], in0=f_x2[:, sl, :],
                                            in1=ps_bc[:].unsqueeze(1).broadcast_to((128, 8, TCH)),
                                            op=ALU.subtract)
                    inc(i, "sV")
                block.vector(l2n)
                t_n2 = ctr.get("sV")
                if t2 == 0:
                    dump("xn2", xn2[:, 0:NHK, :], "sV", t_n2)

                def gmm(t, t2=t2):
                    t.wait_ge(sV, t_n2)
                    i = None
                    for k in range(NHK):
                        i = t.matmul(ps_s1[0:4, :], f_gwT[:, k, :], xn2[:, k, :],
                                     start=(k == 0), stop=(k == NHK - 1))
                    inc(i, "sPE")
                    i = None
                    for k in range(NHK):
                        i = t.matmul(ps_s2[0:32, :], f_laT[:, k, :], xn2[:, k, :],
                                     start=(k == 0), stop=(k == NHK - 1))
                    inc(i, "sPE")
                block.tensor(gmm)
                t_gl = ctr.get("sPE")

                def glg(v, t2=t2):
                    v.wait_ge(sPE, t_gl)
                    inc(v.tensor_scalar(out=f_lg[:], in0=ps_s1[0:4, :], scalar1=r_eb[:, 0:1],
                                        scalar2=None, op0=ALU.add), "sV")
                block.vector(glg)
                t_lg = ctr.get("sV")

                def grw(q, t2=t2):
                    q.wait_ge(sV, t_lg)
                    q.dma_start(out=f_r1[:], in_=f_lg[1:2, :]).then_inc(sQL, 16)
                    q.dma_start(out=f_r2[:], in_=f_lg[2:3, :]).then_inc(sQL, 16)
                    q.dma_start(out=f_tmp[:], in_=f_lg[3:4, :]).then_inc(sQL, 16)
                block.sync(grw)
                t_gr = ctr.add("sQL", 48)

                def gmx(v, t2=t2):
                    v.wait_ge(sQL, t_gr)
                    v.tensor_tensor(out=f_mu[:], in0=f_r1[:], in1=f_lg[0:1, :], op=ALU.max)
                    v.tensor_tensor(out=f_rsd[:], in0=f_r2[:], in1=f_tmp[:], op=ALU.max)
                    v.tensor_tensor(out=f_mu[:], in0=f_mu[:], in1=f_rsd[:], op=ALU.max)
                    v.tensor_tensor(out=f_oh0[:], in0=f_lg[0:1, :], in1=f_mu[:], op=ALU.is_ge)
                    v.tensor_tensor(out=f_oh1[:], in0=f_r1[:], in1=f_mu[:], op=ALU.is_ge)
                    v.tensor_tensor(out=f_oh2[:], in0=f_r2[:], in1=f_mu[:], op=ALU.is_ge)
                    inc(v.tensor_tensor(out=f_oh3[:], in0=f_tmp[:], in1=f_mu[:], op=ALU.is_ge), "sV")
                block.vector(gmx)
                t_oh = ctr.get("sV")
                if t2 == 0:
                    dump("lg", f_lg[:], "sV", t_oh)
                    dump("oh0", f_oh0[:], "sV", t_oh)
                    dump("oh1", f_oh1[:], "sV", t_oh)

                def ohr(q, t2=t2):
                    q.wait_ge(sV, t_oh)
                    for e, oh in enumerate((f_oh0, f_oh1, f_oh2, f_oh3)):
                        q.dma_start(out=d_ohs[e:e + 1, :], in_=oh[:]).then_inc(sQL, 16)
                block.sync(ohr)
                t_ohr0 = ctr.add("sQL", 64)

                def ohr2(q, t2=t2):
                    q.wait_ge(sQL, t_ohr0)
                    q.dma_start(out=f_ohr[:],
                                in_=d_ohs[:, :].unsqueeze(1).broadcast_to((4, 8, TCH))).then_inc(sQL, 16)
                block.sync(ohr2)
                t_ohr = ctr.add("sQL", 16)

                def zg(v, t2=t2):
                    v.wait_ge(sQL, t_ohr)
                    inc(v.tensor_tensor(out=f_zg[:], in0=ps_s2[0:32, :], in1=f_ohr[:], op=ALU.mult), "sV")
                block.vector(zg)
                t_zg = ctr.get("sV")
                if t2 == 0:
                    dump("zg", f_zg[:], "sV", t_zg)

                for pog in range(8):
                    for kq in range(4):
                        par = FWR['par']
                        def wul(q, pog=pog, kq=kq, par=par, t2=t2,
                                t_free=FWR['last'][FWR['par']]):
                            q.wait_ge(sPE, t_free)
                            q.dma_start(out=f_ws[:, par, :, :],
                                        in_=d_wuTb.rearrange("(a p) h -> p a h", p=128)[
                                            :, 8 * kq:8 * (kq + 1),
                                            512 * pog:512 * (pog + 1)]).then_inc(sQL, 16)
                        block.sync(wul)
                        t_wu = ctr.add("sQL", 16)

                        def fmm(t, pog=pog, kq=kq, par=par, t_wu=t_wu, t2=t2):
                            t.wait_ge(sQL, t_wu)
                            if kq == 0:
                                t.wait_ge(sV, ctr.get("sV"))
                            i = None
                            for k in range(8):
                                kk = 8 * kq + k
                                for poi in range(4):
                                    i = t.matmul(PSG[poi][:],
                                                 f_ws[:, par, k, :].rearrange("p (x c) -> p x c", x=4)[:, poi, :],
                                                 xn2[:, kk, :], start=(kk == 0), stop=False)
                            if kq == 3:
                                for poi in range(4):
                                    po = 4 * pog + poi
                                    i = t.matmul(PSG[poi][:], f_lbar[:, 128 * po:128 * (po + 1)], f_zg[:],
                                                 start=False, stop=True)
                            inc(i, "sPE")
                        block.tensor(fmm)
                        FWR['last'][par] = ctr.get("sPE")
                        FWR['par'] = 1 - par
                    t_fm = ctr.get("sPE")

                    def fev(v, pog=pog, t_fm=t_fm, t2=t2):
                        v.wait_ge(sPE, t_fm)
                        i = None
                        for poi in range(4):
                            po = 4 * pog + poi
                            i = v.tensor_tensor(out=f_of[:, poi, :], in0=PSG[poi][:],
                                                in1=f_x2[:, po, :], op=ALU.add)
                        inc(i, "sV")
                    block.vector(fev)
                    t_fe = ctr.get("sV")

                    for q4 in range(4):
                        def ftr(t, pog=pog, q4=q4, t_fe=t_fe, t2=t2):
                            t.wait_ge(sV, t_fe)
                            t.wait_ge(sV, ctr.get("sV"))
                            i = None
                            for poi in range(4):
                                i = t.transpose(ps_x[:, 128 * poi:128 * (poi + 1)],
                                                f_of[:, poi, 128 * q4:128 * (q4 + 1)], r_idf[:])
                            inc(i, "sPE")
                        block.tensor(ftr)
                        t_ft = ctr.get("sPE")

                        def fevc(v, t_ft=t_ft, pog=pog, q4=q4, t2=t2):
                            v.wait_ge(sPE, t_ft)
                            if q4 % 2 == 0:
                                v.wait_ge(sST, ctr.get("sST"))
                            inc(v.tensor_copy(out=f_ost4[:, q4 % 2, :], in_=ps_x[:]), "sV")
                        block.vector(fevc)
                        if q4 % 2 == 1:
                            t_fv = ctr.get("sV")
                            def fst(q, pog=pog, q4=q4, t2=t2, t_fv=t_fv):
                                q.wait_ge(sV, t_fv)
                                base = 512 * t2 + 128 * (q4 - 1)
                                q.dma_start(out=d_out[base:base + 256, 512 * pog:512 * (pog + 1)].rearrange(
                                                "(q p) c -> p q c", p=128),
                                            in_=f_ost4[:]).then_inc(sST, 16)
                            block.sync(fst)
                            ctr.add("sST", 16)

        T_ST = ctr.get("sST")

        def fin(q):
            q.wait_ge(sST, T_ST)
        block.sync(fin)

    return nc


def _rearr_pd(x, X):
    return np.ascontiguousarray(x.reshape(NDT, 128, X).transpose(1, 0, 2).reshape(128, NDT * X))


def kernel(**inputs):
    hs = np.ascontiguousarray(np.asarray(inputs["L0_attn_hidden_states"]).reshape(B * L, H))
    win = np.asarray(inputs["L0_attn_in_proj_w"])
    convw = np.asarray(inputs["L0_attn_conv_w"])
    convb = np.asarray(inputs["L0_attn_conv_b"])
    xpw = np.asarray(inputs["L0_attn_x_proj_w"])
    dtpw = np.asarray(inputs["L0_attn_dt_proj_w"])
    dtbias = np.asarray(inputs["L0_attn_dt_bias"])
    alog = np.asarray(inputs["L0_attn_A_log"])
    dpar = np.asarray(inputs["L0_attn_D_param"])
    woutw = np.asarray(inputs["L0_attn_out_proj_w"])
    dtlnw = np.asarray(inputs["L0_attn_dt_ln_w"])
    blnw = np.asarray(inputs["L0_attn_B_ln_w"])
    clnw = np.asarray(inputs["L0_attn_C_ln_w"])
    wup = np.asarray(inputs["L0_ffn_W_up"])
    gatew = np.asarray(inputs["L0_ffn_gate_w"])
    ebias = np.asarray(inputs["L0_ffn_expert_bias"])
    loraA = np.asarray(inputs["L0_ffn_lora_A"])
    loraB = np.asarray(inputs["L0_ffn_lora_B"])

    xT = np.ascontiguousarray(hs.T)
    woutT = np.ascontiguousarray(woutw.T)
    wupT = np.ascontiguousarray(wup.T)
    gwT = np.ascontiguousarray(gatew.T.reshape(NHK, 128, 4).transpose(1, 0, 2).reshape(128, NHK * 4))
    laT_full = np.ascontiguousarray(loraA.reshape(32, H).T)
    laT = np.ascontiguousarray(laT_full.reshape(NHK, 128, 32).transpose(1, 0, 2).reshape(128, NHK * 32))
    lbar = np.ascontiguousarray(loraB.transpose(0, 2, 1).reshape(32, H))

    in_maps = []
    for c in range(NC):
        b, g = c // 4, c % 4
        ds = slice(DG * g, DG * (g + 1))
        m = dict(
            xTb=xT[:, b * L:(b + 1) * L],
            xTs=np.concatenate([xT[:, 512 * c:512 * (c + 1)],
                                xT[:, L + 512 * c:L + 512 * (c + 1)]], axis=1),
            win=np.concatenate([win[:, DG * g:DG * (g + 1)],
                                win[:, DIN + DG * g:DIN + DG * (g + 1)]], axis=1),
            cw=_rearr_pd(convw[ds], KCV),
            cb=_rearr_pd(convb[ds, None], 1),
            dtb=_rearr_pd(dtbias[ds, None], 1),
            dpar=_rearr_pd(dpar[ds, None], 1),
            alog=_rearr_pd(alog[ds], NST),
            xpwT=_rearr_pd(np.ascontiguousarray(xpw[:, ds].T), ESSM),
            dtpwT=np.ascontiguousarray(
                dtpw[ds].T.reshape(2, 128, DG).transpose(1, 0, 2).reshape(128, 2 * DG)),
            dtlnw=np.ascontiguousarray(dtlnw.reshape(2, 128).T),
            blnw=blnw[:, None],
            clnw=clnw[:, None],
            woT=woutT[1024 * c:1024 * (c + 1)],
            wuT=wupT[512 * c:512 * (c + 1)],
            gwT=gwT,
            laT=laT,
            lbar=lbar,
            eb=ebias[:, None],
        )
        in_maps.append({k: np.ascontiguousarray(v, dtype=np.float32) for k, v in m.items()})

    nc = build_program()
    res = run_bass_kernel_spmd(nc, in_maps, core_ids=list(range(NC)))

    out = np.empty((B * L, H), dtype=np.float32)
    for c in range(NC):
        out[512 * c:512 * (c + 1)] = res.results[c]["out"][0:512]
        out[L + 512 * c:L + 512 * (c + 1)] = res.results[c]["out"][512:1024]
    return out.reshape(B, L, H)


# version 12
# speedup vs baseline: 1.0734x; 1.0734x over previous
import contextlib
import numpy as np
import concourse.bass as bass
import concourse.mybir as mybir
from concourse.bass_utils import run_bass_kernel_spmd
from concourse.aot_env import allow_ici_all_to_all

F32 = mybir.dt.float32
BF16 = mybir.dt.bfloat16
I32 = mybir.dt.int32
AF = mybir.ActivationFunctionType
ALU = mybir.AluOpType

B = 2
L = 4096
H = 4096
DIN = 8192
NST = 16
KCV = 4
RDT = 256
ESSM = 288
NC = 8
DG = 2048
NDT = DG // 128
TCH = 512
NCH = L // TCH
NHK = H // 128
NDK = DIN // 128
EPS_LN = 1e-5
EPS_RMS = 1e-6
DBG = False
G4 = [[0, 1, 2, 3], [4, 5, 6, 7]]
G8 = [list(range(8))]
ENGS = ("sync", "vector", "scalar", "gpsimd", "tensor")


class Ctr:
    def __init__(self):
        self.v = {}

    def add(self, name, n):
        self.v[name] = self.v.get(name, 0) + n
        return self.v[name]

    def get(self, name):
        return self.v.get(name, 0)


def build_program():
    nc = bass.Bass("TRN2", target_bir_lowering=False)

    d_xTb = nc.dram_tensor("xTb", [H, L], F32, kind="ExternalInput").ap()
    d_xTs = nc.dram_tensor("xTs", [H, 1024], F32, kind="ExternalInput").ap()
    d_win = nc.dram_tensor("win", [H, 2 * DG], F32, kind="ExternalInput").ap()
    d_cw = nc.dram_tensor("cw", [128, NDT * KCV], F32, kind="ExternalInput").ap()
    d_cb = nc.dram_tensor("cb", [128, NDT], F32, kind="ExternalInput").ap()
    d_dtb = nc.dram_tensor("dtb", [128, NDT], F32, kind="ExternalInput").ap()
    d_dpar = nc.dram_tensor("dpar", [128, NDT], F32, kind="ExternalInput").ap()
    d_alog = nc.dram_tensor("alog", [128, NDT * NST], F32, kind="ExternalInput").ap()
    d_xpwT = nc.dram_tensor("xpwT", [128, NDT * ESSM], F32, kind="ExternalInput").ap()
    d_dtpwT = nc.dram_tensor("dtpwT", [128, 2 * DG], F32, kind="ExternalInput").ap()
    d_dtlnw = nc.dram_tensor("dtlnw", [128, 2], F32, kind="ExternalInput").ap()
    d_blnw = nc.dram_tensor("blnw", [NST, 1], F32, kind="ExternalInput").ap()
    d_clnw = nc.dram_tensor("clnw", [NST, 1], F32, kind="ExternalInput").ap()
    d_woT = nc.dram_tensor("woT", [DIN // 8, H], F32, kind="ExternalInput").ap()
    d_wuT = nc.dram_tensor("wuT", [H // 8, H], F32, kind="ExternalInput").ap()
    d_gwT = nc.dram_tensor("gwT", [128, NHK * 4], F32, kind="ExternalInput").ap()
    d_laT = nc.dram_tensor("laT", [128, NHK * 32], F32, kind="ExternalInput").ap()
    d_lbar = nc.dram_tensor("lbar", [32, H], F32, kind="ExternalInput").ap()
    d_eb = nc.dram_tensor("eb", [4, 1], F32, kind="ExternalInput").ap()

    d_out = nc.dram_tensor("out", [1024, H], F32, kind="ExternalOutput").ap()

    d_winT = nc.dram_tensor("winTt", [8 * 32 * 128, 512], BF16).ap()
    d_woTb = nc.dram_tensor("woTb", [DIN, H], BF16).ap()
    d_wuTb = nc.dram_tensor("wuTb", [H, H], BF16).ap()
    d_woTsh = nc.dram_tensor("woTsh", [DIN // 8, H], BF16).ap()
    d_wuTsh = nc.dram_tensor("wuTsh", [H // 8, H], BF16).ap()
    d_smp = [nc.dram_tensor(f"smp{c}", [ESSM, TCH], F32).ap() for c in range(NCH)]
    d_smr = [nc.dram_tensor(f"smr{c}", [ESSM, TCH], F32).ap() for c in range(NCH)]
    d_br = nc.dram_tensor("br", [NCH * NST, TCH], BF16).ap()
    d_cr = nc.dram_tensor("cr", [NCH * NST, TCH], BF16).ap()
    d_yin = nc.dram_tensor("yin", [NCH * DG, TCH], BF16).ap()
    d_yrc = nc.dram_tensor("yrc", [NCH * DG, TCH], BF16).ap()
    d_ohs = nc.dram_tensor("ohs", [4, TCH], BF16).ap()

    ctr = Ctr()

    es = contextlib.ExitStack()
    with es:
        r_A = es.enter_context(nc.sbuf_tensor("r_A", [128, NDT * NST], F32))
        r_cw = es.enter_context(nc.sbuf_tensor("r_cw", [128, NDT * KCV], F32))
        r_cb = es.enter_context(nc.sbuf_tensor("r_cb", [128, NDT], F32))
        r_dtb = es.enter_context(nc.sbuf_tensor("r_dtb", [128, NDT], F32))
        r_dpar = es.enter_context(nc.sbuf_tensor("r_dpar", [128, NDT], F32))
        r_dtlnw = es.enter_context(nc.sbuf_tensor("r_dtlnw", [128, 2], F32))
        r_blnw = es.enter_context(nc.sbuf_tensor("r_blnw", [NST, 1], F32))
        r_clnw = es.enter_context(nc.sbuf_tensor("r_clnw", [NST, 1], F32))
        r_eb = es.enter_context(nc.sbuf_tensor("r_eb", [4, 1], F32))
        r_xpwT = es.enter_context(nc.sbuf_tensor("r_xpwT", [128, NDT, ESSM], BF16))
        r_dtpwT = es.enter_context(nc.sbuf_tensor("r_dtpwT", [128, 2, DG], BF16))
        r_ones_b = es.enter_context(nc.sbuf_tensor("r_ones_b", [128, 1], BF16))
        r_ones16_b = es.enter_context(nc.sbuf_tensor("r_ones16_b", [NST, 1], BF16))
        r_onesr_f = es.enter_context(nc.sbuf_tensor("r_onesr_f", [1, 128], F32))
        r_idf = es.enter_context(nc.sbuf_tensor("r_idf", [128, 128], F32))
        r_eps1 = es.enter_context(nc.sbuf_tensor("r_eps1", [128, 1], F32))
        r_eps2 = es.enter_context(nc.sbuf_tensor("r_eps2", [128, 1], F32))
        ps_g0 = es.enter_context(nc.psum_tensor("ps_g0", [128, TCH], F32))
        ps_g1 = es.enter_context(nc.psum_tensor("ps_g1", [128, TCH], F32))
        ps_g2 = es.enter_context(nc.psum_tensor("ps_g2", [128, TCH], F32))
        ps_g3 = es.enter_context(nc.psum_tensor("ps_g3", [128, TCH], F32))
        ps_x = es.enter_context(nc.psum_tensor("ps_x", [128, TCH], F32))
        ps_s1 = es.enter_context(nc.psum_tensor("ps_s1", [128, TCH], F32))
        ps_s2 = es.enter_context(nc.psum_tensor("ps_s2", [128, TCH], F32))
        ps_bc = es.enter_context(nc.psum_tensor("ps_bc", [128, TCH], F32))
        sQL = es.enter_context(nc.semaphore("sQL"))
        sQG = es.enter_context(nc.semaphore("sQG"))
        sCC = es.enter_context(nc.semaphore("sCC"))
        sPE = es.enter_context(nc.semaphore("sPE"))
        sV = es.enter_context(nc.semaphore("sV"))
        sA = es.enter_context(nc.semaphore("sA"))
        sG = es.enter_context(nc.semaphore("sG"))
        sST = es.enter_context(nc.semaphore("sST"))
        sB = es.enter_context(nc.semaphore("sB"))
        block = es.enter_context(nc.Block())
        nc.const_aps.register(F32, EPS_LN, r_eps1[:])
        nc.const_aps.register(F32, EPS_RMS, r_eps2[:])
        PSG = [ps_g0, ps_g1, ps_g2, ps_g3]
        SEM = dict(sQL=sQL, sQG=sQG, sCC=sCC, sPE=sPE, sV=sV, sA=sA, sG=sG, sST=sST, sB=sB)
        SEC = dict(sync=block.sync, vector=block.vector, scalar=block.scalar,
                   gpsimd=block.gpsimd, tensor=block.tensor)

        def inc(instr, sem, n=1):
            instr.then_inc(SEM[sem], n)
            ctr.add(sem, n)
            return ctr.get(sem)

        def dump(name, ap, sem, val):
            if not DBG:
                return
            shp = list(ap.shape)
            d = nc.dram_tensor("dbg_" + name, shp, ap.dtype, kind="ExternalOutput").ap()
            def f(q, d=d, ap=ap, sem=sem, val=val):
                q.wait_ge(SEM[sem], val)
                q.dma_start(out=d[tuple(slice(None) for _ in shp)], in_=ap).then_inc(sST, 16)
            block.sync(f)
            ctr.add("sST", 16)

        def barrier():
            tgt = ctr.get("sB") + 5
            for e in ENGS:
                def f(q, e=e):
                    q.sem_inc(sB, 1)
                    q.wait_ge(sB, tgt)
                SEC[e](f)
            ctr.add("sB", 5)

        with (
            nc.sbuf_tensor("p0_st", [128, 4608], F32) as p0_st,
            nc.sbuf_tensor("p0_stb", [128, 4608], BF16) as p0_stb,
            nc.sbuf_tensor("p0_idi", [128, 128], I32) as p0_idi,
        ):
            def p0_res(q):
                q.dma_start(out=r_A[:], in_=d_alog[:, :]).then_inc(sQL, 16)
                q.dma_start(out=r_cw[:], in_=d_cw[:, :]).then_inc(sQL, 16)
                q.dma_start(out=r_cb[:], in_=d_cb[:, :]).then_inc(sQL, 16)
                q.dma_start(out=r_dtb[:], in_=d_dtb[:, :]).then_inc(sQL, 16)
                q.dma_start(out=r_dpar[:], in_=d_dpar[:, :]).then_inc(sQL, 16)
                q.dma_start(out=r_dtlnw[:], in_=d_dtlnw[:, :]).then_inc(sQL, 16)
                q.dma_start(out=r_blnw[:], in_=d_blnw[:, :]).then_inc(sQL, 16)
                q.dma_start(out=r_clnw[:], in_=d_clnw[:, :]).then_inc(sQL, 16)
                q.dma_start(out=r_eb[:], in_=d_eb[:, :]).then_inc(sQL, 16)
                q.dma_start(out=p0_st[:, 0:NDT * ESSM], in_=d_xpwT[:, :]).then_inc(sQL, 16)
            block.sync(p0_res)
            t0 = ctr.add("sQL", 160)

            def p0_v1(v):
                v.wait_ge(sQL, t0)
                v.memset(r_eps1[:], EPS_LN)
                v.memset(r_eps2[:], EPS_RMS)
                v.memset(r_ones_b[:], 1.0)
                v.memset(r_ones16_b[:], 1.0)
                v.memset(r_onesr_f[:], 1.0)
                inc(v.tensor_copy(out=r_xpwT[:].rearrange("p a e -> p (a e)"),
                                  in_=p0_st[:, 0:NDT * ESSM]), "sV")
            block.vector(p0_v1)
            tv1 = ctr.get("sV")

            def p0_s2(q):
                q.wait_ge(sV, tv1)
                q.dma_start(out=p0_st[:, 0:2 * DG], in_=d_dtpwT[:, :]).then_inc(sQL, 16)
            block.sync(p0_s2)
            t1 = ctr.add("sQL", 16)

            def p0_v2(v):
                v.wait_ge(sQL, t1)
                inc(v.tensor_copy(out=r_dtpwT[:].rearrange("p a d -> p (a d)"),
                                  in_=p0_st[:, 0:2 * DG]), "sV")
            block.vector(p0_v2)

            def p0_sc(s):
                s.wait_ge(sQL, t0)
                inc(s.activation(out=r_A[:], in_=r_A[:], func=AF.Exp), "sA")
            block.scalar(p0_sc)
            ta = ctr.get("sA")

            def p0_v3(v):
                v.wait_ge(sA, ta)
                inc(v.tensor_scalar(out=r_A[:], in0=r_A[:], scalar1=-1.0, scalar2=None,
                                    op0=ALU.mult), "sV")
            block.vector(p0_v3)

            def p0_g(g):
                g.iota(out=p0_idi[:], pattern=[[1, 128]], base=0, channel_multiplier=-1)
                inc(g.tensor_scalar(out=r_idf[:], in0=p0_idi[:], scalar1=0.0, scalar2=None,
                                    op0=ALU.is_equal), "sG")
            block.gpsimd(p0_g)

            for i in range(NHK):
                def ld(q, i=i):
                    q.wait_ge(sA, ctr.get("sA"))
                    q.wait_ge(sV, ctr.get("sV"))
                    q.dma_start(out=p0_st[:, 0:4096], in_=d_win[128 * i:128 * (i + 1), :]).then_inc(sQL, 16)
                block.sync(ld)
                t = ctr.add("sQL", 16)

                def cv(s, t=t, _tg=None):
                    s.wait_ge(sQL, t)
                    s.wait_ge(sQG, ctr.get("sQG"))
                    inc(s.activation(out=p0_stb[:, 0:4096], in_=p0_st[:, 0:4096], func=AF.Copy), "sA")
                block.scalar(cv)
                tc = ctr.get("sA")

                def st(q, i=i, tc=tc):
                    q.wait_ge(sA, tc)
                    for pog in range(8):
                        base = (pog * 32 + i) * 128
                        q.dma_start(out=d_winT[base:base + 128, :],
                                    in_=p0_stb[:, 512 * pog:512 * (pog + 1)]).then_inc(sQG, 16)
                block.gpsimd(st)
                ctr.add("sQG", 8 * 16)

            T_WINT = ctr.get("sQG")

            for i in range(DIN // 8 // 128):
                def ld(q, i=i):
                    q.wait_ge(sA, ctr.get("sA"))
                    q.dma_start(out=p0_st[:, 0:4096], in_=d_woT[128 * i:128 * (i + 1), :]).then_inc(sQL, 16)
                block.sync(ld)
                t = ctr.add("sQL", 16)
                def cv(s, t=t, _tg=None):
                    s.wait_ge(sQL, t)
                    s.wait_ge(sQG, ctr.get("sQG"))
                    inc(s.activation(out=p0_stb[:, 0:4096], in_=p0_st[:, 0:4096], func=AF.Copy), "sA")
                block.scalar(cv)
                tc = ctr.get("sA")
                def st(q, i=i, tc=tc):
                    q.wait_ge(sA, tc)
                    q.dma_start(out=d_woTsh[128 * i:128 * (i + 1), :], in_=p0_stb[:, 0:4096]).then_inc(sQG, 16)
                block.gpsimd(st)
                ctr.add("sQG", 16)
            for i in range(H // 8 // 128):
                def ld(q, i=i):
                    q.wait_ge(sA, ctr.get("sA"))
                    q.dma_start(out=p0_st[:, 0:4096], in_=d_wuT[128 * i:128 * (i + 1), :]).then_inc(sQL, 16)
                block.sync(ld)
                t = ctr.add("sQL", 16)
                def cv(s, t=t, _tg=None):
                    s.wait_ge(sQL, t)
                    s.wait_ge(sQG, ctr.get("sQG"))
                    inc(s.activation(out=p0_stb[:, 0:4096], in_=p0_st[:, 0:4096], func=AF.Copy), "sA")
                block.scalar(cv)
                tc = ctr.get("sA")
                def st(q, i=i, tc=tc):
                    q.wait_ge(sA, tc)
                    q.dma_start(out=d_wuTsh[128 * i:128 * (i + 1), :], in_=p0_stb[:, 0:4096]).then_inc(sQG, 16)
                block.gpsimd(st)
                ctr.add("sQG", 16)

            T_WSH = ctr.get("sQG")
        T_P0CC = None

        barrier()

        with contextlib.ExitStack() as es1:
            m_xn1 = es1.enter_context(nc.sbuf_tensor("m_xn1", [128, NHK, TCH], BF16))
            m_xc = es1.enter_context(nc.sbuf_tensor("m_xc", [128, NDT, TCH], BF16))
            m_zs = es1.enter_context(nc.sbuf_tensor("m_zs", [128, NDT, TCH], BF16))
            m_brep = es1.enter_context(nc.sbuf_tensor("m_brep", [128, NST * TCH], BF16))
            m_crep = es1.enter_context(nc.sbuf_tensor("m_crep", [128, NST * TCH], BF16))
            m_a = es1.enter_context(nc.sbuf_tensor("m_a", [128, 2, 4 * TCH], F32))
            m_w = es1.enter_context(nc.sbuf_tensor("m_w", [128, 4 * TCH], BF16))
            m_h = es1.enter_context(nc.sbuf_tensor("m_h", [128, 4 * TCH], F32))
            m_hc = es1.enter_context(nc.sbuf_tensor("m_hc", [128, 4 * TCH], BF16))
            m_th = es1.enter_context(nc.sbuf_tensor("m_th", [128, 4, TCH], BF16))
            m_ws = es1.enter_context(nc.sbuf_tensor("m_ws", [128, 2, 8, TCH], BF16))
            m_pad = es1.enter_context(nc.sbuf_tensor("m_pad", [128, KCV - 1 + TCH], F32))
            m_cva = es1.enter_context(nc.sbuf_tensor("m_cva", [128, TCH], F32))
            m_cvb = es1.enter_context(nc.sbuf_tensor("m_cvb", [128, TCH], F32))
            m_smp = es1.enter_context(nc.sbuf_tensor("m_smp", [128, 3, TCH], F32))
            m_dtr = es1.enter_context(nc.sbuf_tensor("m_dtr", [128, 2, TCH], BF16))
            m_bc = es1.enter_context(nc.sbuf_tensor("m_bc", [NST, 2, TCH], F32))
            m_bcn = es1.enter_context(nc.sbuf_tensor("m_bcn", [NST, 2, TCH], BF16))
            m_dtsp = es1.enter_context(nc.sbuf_tensor("m_dtsp", [128, 2, TCH], F32))
            m_dtx = es1.enter_context(nc.sbuf_tensor("m_dtx", [128, TCH], BF16))
            m_sq = es1.enter_context(nc.sbuf_tensor("m_sq", [128, TCH], BF16))
            m_sq2 = es1.enter_context(nc.sbuf_tensor("m_sq2", [128, TCH], BF16))
            m_mu = es1.enter_context(nc.sbuf_tensor("m_mu", [1, TCH], F32))
            m_rsd = es1.enter_context(nc.sbuf_tensor("m_rsd", [1, TCH], F32))
            m_tmp = es1.enter_context(nc.sbuf_tensor("m_tmp", [1, TCH], F32))
            m_carry = es1.enter_context(nc.sbuf_tensor("m_carry", [128, NDT * NST], F32))
            m_halo = es1.enter_context(nc.sbuf_tensor("m_halo", [128, NDT, KCV - 1], F32))
            m_yout = es1.enter_context(nc.sbuf_tensor("m_yout", [128, 1, TCH], BF16))
            xs = m_a[:, 0, :].rearrange("p (a t) -> p a t", a=4)

            WSR = {'par': 0, 'last': [0, 0]}
            AR_ = {'par': 0, 'last': [0, 0]}
            YR_ = {'par': 0, 'last': [0, 0]}

            def m_init(v):
                v.memset(m_carry[:], 0.0)
                inc(v.memset(m_halo[:].rearrange("p a k -> p (a k)"), 0.0), "sV")
            block.vector(m_init)

            for ch in range(NCH):
                for grp in range(8):
                    def xl(q, grp=grp, ch=ch):
                        q.wait_ge(sV, ctr.get("sV"))
                        q.wait_ge(sA, ctr.get("sA"))
                        q.dma_start(out=xs[:, :, :],
                                    in_=d_xTb.rearrange("(a p) t -> p a t", p=128)[
                                        :, 4 * grp:4 * (grp + 1), TCH * ch:TCH * (ch + 1)]).then_inc(sQL, 16)
                    block.sync(xl)
                    t_x = ctr.add("sQL", 16)

                    def xc_cast(v, grp=grp, t_x=t_x):
                        v.wait_ge(sQL, t_x)
                        v.wait_ge(sPE, ctr.get("sPE"))
                        inc(v.tensor_copy(out=m_xn1[:, 4 * grp:4 * (grp + 1), :].rearrange("p a t -> p (a t)"),
                                          in_=xs[:, :, :].rearrange("p a t -> p (a t)")), "sV")
                    block.vector(xc_cast)
                    t_c = ctr.get("sV")

                    def sqg(s, grp=grp, t_c=t_c):
                        s.wait_ge(sV, t_c)
                        s.wait_ge(sPE, ctr.get("sPE"))
                        i = None
                        for j in range(4):
                            i = s.activation(out=m_th[:, j, :], in_=xs[:, j, :], func=AF.Square)
                        inc(i, "sA")
                    block.scalar(sqg)
                    t_s = ctr.get("sA")

                    def mmg4(t, grp=grp, t_c=t_c, t_s=t_s, ch=ch):
                        t.wait_ge(sV, t_c)
                        t.wait_ge(sA, t_s)
                        i = None
                        for j in range(4):
                            k = 4 * grp + j
                            t.matmul(ps_s1[0:1, :], r_ones_b[:], m_xn1[:, k, :],
                                     start=(k == 0), stop=(k == NHK - 1))
                            i = t.matmul(ps_s2[0:1, :], r_ones_b[:], m_th[:, j, :],
                                         start=(k == 0), stop=(k == NHK - 1))
                        inc(i, "sPE")
                    block.tensor(mmg4)
                t_st = ctr.get("sPE")

                def ln_f1(v, ch=ch):
                    v.wait_ge(sPE, t_st)
                    v.tensor_scalar(out=m_mu[:], in0=ps_s1[0:1, :], scalar1=1.0 / H,
                                    scalar2=None, op0=ALU.mult)
                    inc(v.tensor_scalar(out=m_tmp[:], in0=ps_s2[0:1, :], scalar1=1.0 / H,
                                        scalar2=None, op0=ALU.mult), "sV")
                block.vector(ln_f1)
                t_f1 = ctr.get("sV")

                def ln_f2(s, ch=ch):
                    s.wait_ge(sV, t_f1)
                    inc(s.activation(out=m_rsd[:], in_=m_mu[:], func=AF.Square), "sA")
                block.scalar(ln_f2)
                t_f2 = ctr.get("sA")

                def ln_f3(v, ch=ch):
                    v.wait_ge(sA, t_f2)
                    v.tensor_tensor(out=m_tmp[:], in0=m_tmp[:], in1=m_rsd[:], op=ALU.subtract)
                    inc(v.tensor_scalar(out=m_mu[:], in0=m_mu[:], scalar1=-1.0, scalar2=None,
                                        op0=ALU.mult), "sV")
                block.vector(ln_f3)
                t_f3 = ctr.get("sV")

                def ln_f4(s, ch=ch):
                    s.wait_ge(sV, t_f3)
                    s.activation(out=m_tmp[:], in_=m_tmp[:], func=AF.Ln, bias=EPS_LN)
                    inc(s.activation(out=m_rsd[:], in_=m_tmp[:], func=AF.Exp, scale=-0.5), "sA")
                block.scalar(ln_f4)
                t_f4 = ctr.get("sA")

                def ln_bc(t, ch=ch):
                    t.wait_ge(sA, t_f4)
                    t.wait_ge(sG, 1)
                    t.matmul(ps_bc[:], r_onesr_f[:], m_mu[:], start=True, stop=True)
                    inc(t.matmul(ps_x[:], r_onesr_f[:], m_rsd[:], start=True, stop=True), "sPE")
                block.tensor(ln_bc)
                t_lbc = ctr.get("sPE")

                for grp in range(8):
                    def xl2(q, grp=grp, ch=ch):
                        q.wait_ge(sV, ctr.get("sV"))
                        q.dma_start(out=xs[:, :, :],
                                    in_=d_xTb.rearrange("(a p) t -> p a t", p=128)[
                                        :, 4 * grp:4 * (grp + 1), TCH * ch:TCH * (ch + 1)]).then_inc(sQL, 16)
                    block.sync(xl2)
                    t_x2 = ctr.add("sQL", 16)

                    def nrm(v, grp=grp, t_x2=t_x2, ch=ch):
                        v.wait_ge(sQL, t_x2)
                        v.wait_ge(sPE, t_lbc)
                        v.tensor_tensor(out=xs[:, :, :], in0=xs[:, :, :],
                                        in1=ps_bc[:].unsqueeze(1).broadcast_to((128, 4, TCH)), op=ALU.add)
                        inc(v.tensor_tensor(out=m_xn1[:, 4 * grp:4 * (grp + 1), :],
                                            in0=xs[:, :, :],
                                            in1=ps_x[:].unsqueeze(1).broadcast_to((128, 4, TCH)),
                                            op=ALU.mult), "sV")
                    block.vector(nrm)
                t_xn1 = ctr.get("sV")
                if ch == 0:
                    dump("xn1", m_xn1[:], "sV", t_xn1)
                    dump("mu", m_mu[:], "sV", t_xn1)
                    dump("rsd", m_rsd[:], "sV", t_xn1)

                for pog in range(4):
                    for kq in range(4):
                        par = WSR['par']
                        def wld(q, pog=pog, kq=kq, par=par, ch=ch,
                                t_free=WSR['last'][WSR['par']]):
                            q.wait_ge(sPE, t_free)
                            q.wait_ge(sQG, T_WINT)
                            base = (pog * 32 + 8 * kq) * 128
                            q.dma_start(out=m_ws[:, par, :, :],
                                        in_=d_winT[base:base + 8 * 128, :].rearrange("(a p) c -> p a c", p=128)).then_inc(sQL, 16)
                        block.sync(wld)
                        t_w = ctr.add("sQL", 16)

                        def mmg(t, pog=pog, kq=kq, par=par, t_w=t_w, ch=ch):
                            t.wait_ge(sQL, t_w)
                            t.wait_ge(sV, t_xn1)
                            if kq == 0:
                                t.wait_ge(sA, ctr.get("sA"))
                                t.wait_ge(sV, ctr.get("sV"))
                            i = None
                            for k in range(8):
                                kk = 8 * kq + k
                                for poi in range(4):
                                    i = t.matmul(PSG[poi][:],
                                                 m_ws[:, par, k, :].rearrange("p (x c) -> p x c", x=4)[:, poi, :],
                                                 m_xn1[:, kk, :], start=(kk == 0), stop=(kk == 31))
                            inc(i, "sPE")
                        block.tensor(mmg)
                        WSR['last'][par] = ctr.get("sPE")
                        WSR['par'] = 1 - par
                    t_mm = ctr.get("sPE")

                    for poi in range(4):
                        dtile = 4 * (pog % 4) + poi
                        if True:
                            def cpad(s, poi=poi, t_mm=t_mm, ch=ch):
                                s.wait_ge(sPE, t_mm)
                                s.wait_ge(sV, ctr.get("sV"))
                                inc(s.copy(out=m_pad[:, KCV - 1:], in_=PSG[poi][:]), "sA")
                            block.scalar(cpad)
                            t_p = ctr.get("sA")

                            def ctaps(v, dtile=dtile, t_p=t_p, ch=ch):
                                v.wait_ge(sA, t_p)
                                v.tensor_copy(out=m_pad[:, 0:KCV - 1], in_=m_halo[:, dtile, :])
                                v.tensor_copy(out=m_halo[:, dtile, :], in_=m_pad[:, TCH:TCH + KCV - 1])
                                v.tensor_scalar(out=m_cva[:], in0=m_pad[:, 0:TCH],
                                                scalar1=r_cw[:, KCV * dtile:KCV * dtile + 1],
                                                scalar2=r_cb[:, dtile:dtile + 1],
                                                op0=ALU.mult, op1=ALU.add)
                                v.scalar_tensor_tensor(out=m_cvb[:], in0=m_pad[:, 1:1 + TCH],
                                                       scalar=r_cw[:, KCV * dtile + 1:KCV * dtile + 2],
                                                       in1=m_cva[:], op0=ALU.mult, op1=ALU.add)
                                v.scalar_tensor_tensor(out=m_cva[:], in0=m_pad[:, 2:2 + TCH],
                                                       scalar=r_cw[:, KCV * dtile + 2:KCV * dtile + 3],
                                                       in1=m_cvb[:], op0=ALU.mult, op1=ALU.add)
                                inc(v.scalar_tensor_tensor(out=m_cvb[:], in0=m_pad[:, 3:3 + TCH],
                                                           scalar=r_cw[:, KCV * dtile + 3:KCV * dtile + 4],
                                                           in1=m_cva[:], op0=ALU.mult, op1=ALU.add), "sV")
                            block.vector(ctaps)
                            t_t = ctr.get("sV")

                            def csilu(s, dtile=dtile, t_t=t_t, ch=ch):
                                s.wait_ge(sV, t_t)
                                inc(s.activation(out=m_xc[:, dtile, :], in_=m_cvb[:], func=AF.Silu), "sA")
                            block.scalar(csilu)
                t_inproj = ctr.get("sA")
                if ch == 0:
                    dump("xc", m_xc[:], "sA", t_inproj)

                for epo in range(3):
                    rows = 128 if epo < 2 else 32
                    def xpm(t, epo=epo, rows=rows, ch=ch):
                        t.wait_ge(sA, t_inproj)
                        if epo > 0:
                            t.wait_ge(sV, ctr.get("sV"))
                        i = None
                        for dtile in range(NDT):
                            i = t.matmul(ps_x[0:rows, :],
                                         r_xpwT[:, dtile, 128 * epo:128 * epo + rows],
                                         m_xc[:, dtile, :], start=(dtile == 0), stop=(dtile == NDT - 1))
                        inc(i, "sPE")
                    block.tensor(xpm)
                    t_xp = ctr.get("sPE")

                    def xpe(v, epo=epo, rows=rows, t_xp=t_xp, ch=ch):
                        v.wait_ge(sPE, t_xp)
                        inc(v.tensor_copy(out=m_smp[0:rows, epo, :], in_=ps_x[0:rows, :]), "sV")
                    block.vector(xpe)
                t_xpe = ctr.get("sV")

                def sst(q, ch=ch):
                    q.wait_ge(sV, t_xpe)
                    q.dma_start(out=d_smp[ch][0:128, :], in_=m_smp[:, 0, :]).then_inc(sQL, 16)
                    q.dma_start(out=d_smp[ch][128:256, :], in_=m_smp[:, 1, :]).then_inc(sQL, 16)
                    q.dma_start(out=d_smp[ch][256:288, :], in_=m_smp[0:32, 2, :]).then_inc(sQL, 16)
                block.sync(sst)
                t_sst = ctr.add("sQL", 48)

                def scc(g, ch=ch):
                    g.wait_ge(sQL, t_sst)
                    g.collective_compute("AllReduce", ALU.add, replica_groups=G4,
                                         ins=[d_smp[ch].opt()], outs=[d_smr[ch].opt()]).then_inc(sCC)
                block.gpsimd(scc)
                t_scc = ctr.add("sCC", 1)

                for pog in range(4, 8):
                    for kq in range(4):
                        par = WSR['par']
                        def wldz(q, pog=pog, kq=kq, par=par, ch=ch,
                                 t_free=WSR['last'][WSR['par']]):
                            q.wait_ge(sPE, t_free)
                            q.wait_ge(sQG, T_WINT)
                            base = (pog * 32 + 8 * kq) * 128
                            q.dma_start(out=m_ws[:, par, :, :],
                                        in_=d_winT[base:base + 8 * 128, :].rearrange("(a p) c -> p a c", p=128)).then_inc(sQL, 16)
                        block.sync(wldz)
                        t_w = ctr.add("sQL", 16)

                        def mmgz(t, pog=pog, kq=kq, par=par, t_w=t_w, ch=ch):
                            t.wait_ge(sQL, t_w)
                            if kq == 0:
                                t.wait_ge(sA, ctr.get("sA"))
                                t.wait_ge(sV, ctr.get("sV"))
                            i = None
                            for k in range(8):
                                kk = 8 * kq + k
                                for poi in range(4):
                                    i = t.matmul(PSG[poi][:],
                                                 m_ws[:, par, k, :].rearrange("p (x c) -> p x c", x=4)[:, poi, :],
                                                 m_xn1[:, kk, :], start=(kk == 0), stop=(kk == 31))
                            inc(i, "sPE")
                        block.tensor(mmgz)
                        WSR['last'][par] = ctr.get("sPE")
                        WSR['par'] = 1 - par
                    t_mmz = ctr.get("sPE")
                    for poi in range(4):
                        dtile = 4 * (pog % 4) + poi
                        def zsilu(s, dtile=dtile, poi=poi, t_mmz=t_mmz, ch=ch):
                            s.wait_ge(sPE, t_mmz)
                            inc(s.activation(out=m_zs[:, dtile, :], in_=PSG[poi][:], func=AF.Silu), "sA")
                        block.scalar(zsilu)
                if ch == 0:
                    dump("zs", m_zs[:], "sA", ctr.get("sA"))

                def sld(q, ch=ch):
                    q.wait_ge(sCC, t_scc)
                    q.wait_ge(sV, ctr.get("sV"))
                    q.dma_start(out=m_smp[:, 0, :], in_=d_smr[ch][0:128, :]).then_inc(sQL, 16)
                    q.dma_start(out=m_smp[:, 1, :], in_=d_smr[ch][128:256, :]).then_inc(sQL, 16)
                    q.dma_start(out=m_bc[:, 0, :], in_=d_smr[ch][256:272, :]).then_inc(sQL, 16)
                    q.dma_start(out=m_bc[:, 1, :], in_=d_smr[ch][272:288, :]).then_inc(sQL, 16)
                block.sync(sld)
                t_sld = ctr.add("sQL", 64)

                def rsq(s, ch=ch):
                    s.wait_ge(sQL, t_sld)
                    s.wait_ge(sPE, ctr.get("sPE"))
                    s.activation(out=m_sq[:], in_=m_smp[:, 0, :], func=AF.Square)
                    inc(s.activation(out=m_sq2[:], in_=m_smp[:, 1, :], func=AF.Square), "sA")
                block.scalar(rsq)
                t_r1 = ctr.get("sA")

                def rmm(t, ch=ch):
                    t.wait_ge(sA, t_r1)
                    t.matmul(ps_s1[0:1, :], r_ones_b[:], m_sq[:], start=True, stop=False)
                    inc(t.matmul(ps_s1[0:1, :], r_ones_b[:], m_sq2[:], start=False, stop=True), "sPE")
                block.tensor(rmm)
                t_r2 = ctr.get("sPE")

                def rsqbc(s, ch=ch):
                    s.wait_ge(sQL, t_sld)
                    s.activation(out=m_sq[0:NST, :], in_=m_bc[:, 0, :], func=AF.Square)
                    inc(s.activation(out=m_sq2[0:NST, :], in_=m_bc[:, 1, :], func=AF.Square), "sA")
                block.scalar(rsqbc)
                t_r3 = ctr.get("sA")

                def rmmbc(t, ch=ch):
                    t.wait_ge(sA, t_r3)
                    t.matmul(ps_s2[0:1, :], r_ones16_b[:], m_sq[0:NST, :], start=True, stop=True)
                    inc(t.matmul(ps_bc[0:1, :], r_ones16_b[:], m_sq2[0:NST, :], start=True, stop=True), "sPE")
                block.tensor(rmmbc)
                t_r4 = ctr.get("sPE")

                def rrsd(s, ch=ch):
                    s.wait_ge(sPE, max(t_r2, t_r4))
                    s.activation(out=m_mu[:], in_=ps_s1[0:1, :], func=AF.Ln, scale=1.0 / RDT, bias=EPS_RMS)
                    s.activation(out=m_mu[:], in_=m_mu[:], func=AF.Exp, scale=-0.5)
                    s.activation(out=m_rsd[:], in_=ps_s2[0:1, :], func=AF.Ln, scale=1.0 / NST, bias=EPS_RMS)
                    s.activation(out=m_rsd[:], in_=m_rsd[:], func=AF.Exp, scale=-0.5)
                    s.activation(out=m_tmp[:], in_=ps_bc[0:1, :], func=AF.Ln, scale=1.0 / NST, bias=EPS_RMS)
                    inc(s.activation(out=m_tmp[:], in_=m_tmp[:], func=AF.Exp, scale=-0.5), "sA")
                block.scalar(rrsd)
                t_r5 = ctr.get("sA")

                def rbc(t, ch=ch):
                    t.wait_ge(sA, t_r5)
                    t.matmul(ps_bc[:], r_onesr_f[:], m_mu[:], start=True, stop=True)
                    t.matmul(ps_s1[0:NST, :], r_onesr_f[0:1, 0:NST], m_rsd[:], start=True, stop=True)
                    inc(t.matmul(ps_s2[0:NST, :], r_onesr_f[0:1, 0:NST], m_tmp[:], start=True, stop=True), "sPE")
                block.tensor(rbc)
                t_r6 = ctr.get("sPE")

                def rap(v, ch=ch):
                    v.wait_ge(sPE, t_r6)
                    v.scalar_tensor_tensor(out=m_dtr[:, 0, :], in0=m_smp[:, 0, :], scalar=r_dtlnw[:, 0:1],
                                           in1=ps_bc[:], op0=ALU.mult, op1=ALU.mult)
                    v.scalar_tensor_tensor(out=m_dtr[:, 1, :], in0=m_smp[:, 1, :], scalar=r_dtlnw[:, 1:2],
                                           in1=ps_bc[:], op0=ALU.mult, op1=ALU.mult)
                    v.scalar_tensor_tensor(out=m_bcn[:, 0, :], in0=m_bc[:, 0, :], scalar=r_blnw[:, 0:1],
                                           in1=ps_s1[0:NST, :], op0=ALU.mult, op1=ALU.mult)
                    inc(v.scalar_tensor_tensor(out=m_bcn[:, 1, :], in0=m_bc[:, 1, :], scalar=r_clnw[:, 0:1],
                                               in1=ps_s2[0:NST, :], op0=ALU.mult, op1=ALU.mult), "sV")
                block.vector(rap)
                t_r7 = ctr.get("sV")
                if ch == 0:
                    dump("smp", m_smp[:], "sV", t_r7)
                    dump("dtr", m_dtr[:], "sV", t_r7)
                    dump("bcn", m_bcn[:], "sV", t_r7)

                def bst(q, ch=ch):
                    q.wait_ge(sV, t_r7)
                    q.dma_start(out=d_br[NST * ch:NST * (ch + 1), :], in_=m_bcn[:, 0, :]).then_inc(sQL, 16)
                    q.dma_start(out=d_cr[NST * ch:NST * (ch + 1), :], in_=m_bcn[:, 1, :]).then_inc(sQL, 16)
                block.sync(bst)
                t_bst = ctr.add("sQL", 32)

                def brl(q, ch=ch):
                    q.wait_ge(sQL, t_bst)
                    q.wait_ge(sV, ctr.get("sV"))
                    q.wait_ge(sG, ctr.get("sG"))
                    q.dma_start(out=m_brep[:],
                                in_=d_br[NST * ch:NST * (ch + 1), :].rearrange("n t -> (n t)").unsqueeze(0).broadcast_to((128, NST * TCH))).then_inc(sQL, 16)
                    q.dma_start(out=m_crep[:],
                                in_=d_cr[NST * ch:NST * (ch + 1), :].rearrange("n t -> (n t)").unsqueeze(0).broadcast_to((128, NST * TCH))).then_inc(sQL, 16)
                block.sync(brl)
                t_brl = ctr.add("sQL", 32)

                for dtile in range(NDT):
                    dts = m_dtsp[:, dtile % 2, :]
                    def dpm(t, dtile=dtile, ch=ch):
                        t.wait_ge(sV, t_r7)
                        t.wait_ge(sA, ctr.get("sA"))
                        t.matmul(ps_x[:], r_dtpwT[:, 0, 128 * dtile:128 * (dtile + 1)], m_dtr[:, 0, :],
                                 start=True, stop=False)
                        inc(t.matmul(ps_x[:], r_dtpwT[:, 1, 128 * dtile:128 * (dtile + 1)], m_dtr[:, 1, :],
                                     start=False, stop=True), "sPE")
                    block.tensor(dpm)
                    t_d1 = ctr.get("sPE")

                    def sp0(v, dtile=dtile, t_d1=t_d1, ch=ch):
                        v.wait_ge(sPE, t_d1)
                        v.wait_ge(sA, ctr.get("sA"))
                        v.tensor_scalar(out=m_cvb[:], in0=ps_x[:], scalar1=r_dtb[:, dtile:dtile + 1],
                                        scalar2=None, op0=ALU.add)
                        v.tensor_scalar(out=m_cva[:], in0=m_cvb[:], scalar1=-1.0, scalar2=None, op0=ALU.mult)
                        v.tensor_tensor(out=m_cva[:], in0=m_cva[:], in1=m_cvb[:], op=ALU.max)
                        inc(v.tensor_scalar(out=m_cvb[:], in0=m_cvb[:], scalar1=0.0,
                                            scalar2=None, op0=ALU.max), "sV")
                    block.vector(sp0)
                    t_d1v = ctr.get("sV")

                    def sp(s, dtile=dtile, dts=dts, t_d1v=t_d1v, ch=ch):
                        s.wait_ge(sV, t_d1v)
                        s.activation(out=m_cva[:], in_=m_cva[:], func=AF.Exp, scale=-1.0)
                        inc(s.activation(out=m_cva[:], in_=m_cva[:], func=AF.Ln, bias=1.0), "sA")
                    block.scalar(sp)
                    t_d2 = ctr.get("sA")

                    def spa(v, dts=dts, dtile=dtile, t_d2=t_d2, ch=ch):
                        v.wait_ge(sA, t_d2)
                        v.tensor_tensor(out=dts, in0=m_cva[:], in1=m_cvb[:], op=ALU.add)
                        inc(v.tensor_tensor(out=m_dtx[:], in0=dts, in1=m_xc[:, dtile, :], op=ALU.mult), "sV")
                    block.vector(spa)
                    t_d3 = ctr.get("sV")
                    if ch == 0 and dtile == 0:
                        dump("dtsp", dts, "sV", t_d3)
                        dump("dtx", m_dtx[:], "sV", t_d3)

                    for quar in range(4):
                        nb = 4 * quar
                        apar = AR_['par']
                        def aexp(s, dtile=dtile, nb=nb, dts=dts, t_d3=t_d3, ch=ch, apar=apar,
                                 t_free=AR_['last'][AR_['par']]):
                            s.wait_ge(sV, t_d3)
                            s.wait_ge(sV, t_free)
                            i = None
                            for j in range(4):
                                n = NST * dtile + nb + j
                                i = s.activation(out=m_a[:, apar, TCH * j:TCH * (j + 1)], in_=dts,
                                                 func=AF.Exp, scale=r_A[:, n:n + 1])
                            inc(i, "sA")
                        block.scalar(aexp)
                        t_h1 = ctr.get("sA")

                        def wm(v, nb=nb, t_h1=t_h1, ch=ch, dtile=dtile):
                            v.wait_ge(sQL, t_brl)
                            v.wait_ge(sV, t_d3)
                            i = None
                            for j in range(4):
                                i = v.tensor_tensor(out=m_w[:, TCH * j:TCH * (j + 1)], in0=m_dtx[:],
                                                    in1=m_brep[:, TCH * (nb + j):TCH * (nb + j + 1)],
                                                    op=ALU.mult)
                            inc(i, "sV")
                        block.vector(wm)
                        t_h2 = ctr.get("sV")

                        def scans(v, dtile=dtile, nb=nb, t_h1=t_h1, ch=ch, apar=apar):
                            v.wait_ge(sA, t_h1)
                            v.wait_ge(sG, ctr.get("sG"))
                            i = None
                            for j in range(4):
                                n = NST * dtile + nb + j
                                i = v.tensor_tensor_scan(out=m_h[:, TCH * j:TCH * (j + 1)],
                                                         data0=m_a[:, apar, TCH * j:TCH * (j + 1)],
                                                         data1=m_w[:, TCH * j:TCH * (j + 1)],
                                                         initial=m_carry[:, n:n + 1],
                                                         op0=ALU.mult, op1=ALU.add)
                            inc(i, "sV")
                        block.vector(scans)
                        t_h3 = ctr.get("sV")
                        AR_['last'][apar] = ctr.get("sV")
                        AR_['par'] = 1 - apar
                        if ch == 0 and dtile == 0 and quar == 0:
                            dump("sa", m_a[:], "sV", t_h3)
                            dump("sw", m_w[:], "sV", t_h3)
                            dump("sh", m_h[:], "sV", t_h3)

                        def cex(s, dtile=dtile, nb=nb, t_h3=t_h3, ch=ch):
                            s.wait_ge(sV, t_h3)
                            n0 = NST * dtile + nb
                            inc(s.copy(out=m_carry[:, n0:n0 + 4], in_=m_h[:, TCH - 1::TCH]), "sA")
                        block.scalar(cex)

                        def hc(g, nb=nb, t_h3=t_h3, ch=ch, dtile=dtile):
                            g.wait_ge(sV, t_h3)
                            inc(g.tensor_tensor(out=m_hc[:], in0=m_h[:],
                                                in1=m_crep[:, TCH * nb:TCH * (nb + 4)], op=ALU.mult), "sG")
                        block.gpsimd(hc)
                        t_h4 = ctr.get("sG")

                        def tre(v, quar=quar, t_h4=t_h4, ch=ch, dtile=dtile):
                            v.wait_ge(sG, t_h4)
                            v.tensor_tensor(out=m_hc[:, 0:2 * TCH], in0=m_hc[:, 0:2 * TCH],
                                            in1=m_hc[:, 2 * TCH:4 * TCH], op=ALU.add)
                            inc(v.tensor_tensor(out=m_th[:, quar, :], in0=m_hc[:, 0:TCH],
                                                in1=m_hc[:, TCH:2 * TCH], op=ALU.add), "sV")
                        block.vector(tre)

                    ypar = 0
                    def yasm(v, dtile=dtile, ch=ch, ypar=ypar, t_free=ctr.get("sQG")):
                        v.wait_ge(sV, ctr.get("sV"))
                        v.wait_ge(sQG, t_free)
                        v.tensor_tensor(out=m_cva[:], in0=m_th[:, 0, :], in1=m_th[:, 1, :], op=ALU.add)
                        v.tensor_tensor(out=m_cvb[:], in0=m_th[:, 2, :], in1=m_th[:, 3, :], op=ALU.add)
                        v.tensor_tensor(out=m_cva[:], in0=m_cva[:], in1=m_cvb[:], op=ALU.add)
                        v.scalar_tensor_tensor(out=m_cvb[:], in0=m_xc[:, dtile, :],
                                               scalar=r_dpar[:, dtile:dtile + 1],
                                               in1=m_cva[:], op0=ALU.mult, op1=ALU.add)
                        inc(v.tensor_tensor(out=m_yout[:, ypar, :], in0=m_cvb[:], in1=m_zs[:, dtile, :],
                                            op=ALU.mult), "sV")
                    block.vector(yasm)
                    t_y = ctr.get("sV")

                    if ch == 0 and dtile == 0:
                        dump("yout", m_yout[:, ypar, :], "sV", t_y)
                    def yst(q, dtile=dtile, ch=ch, t_y=t_y, ypar=ypar):
                        q.wait_ge(sV, t_y)
                        r0 = DG * ch + 128 * dtile
                        q.dma_start(out=d_yin[r0:r0 + 128, :], in_=m_yout[:, ypar, :]).then_inc(sQG, 16)
                    block.gpsimd(yst)
                    YR_['last'][ypar] = ctr.add("sQG", 16)
                    YR_['par'] = 1 - ypar

            def wcc(g):
                g.wait_ge(sQG, T_WSH)
                g.collective_compute("AllGather", ALU.bypass, replica_groups=G8,
                                     ins=[d_woTsh.opt()], outs=[d_woTb.opt()]).then_inc(sCC)
                g.collective_compute("AllGather", ALU.bypass, replica_groups=G8,
                                     ins=[d_wuTsh.opt()], outs=[d_wuTb.opt()]).then_inc(sCC)
            block.gpsimd(wcc)
            ctr.add("sCC", 2)
            T_P0CC = ctr.get("sCC")

            t_yall = ctr.get("sQG")

            def ycc(g):
                g.wait_ge(sQG, t_yall)
                with allow_ici_all_to_all():
                    g.collective_compute("AllToAll", ALU.bypass, replica_groups=G8,
                                         ins=[d_yin.opt()], outs=[d_yrc.opt()]).then_inc(sCC)
            block.gpsimd(ycc)
            ctr.add("sCC", 1)
        T_P1CC = ctr.get("sCC")

        barrier()

        with contextlib.ExitStack() as es2:
            f_x2 = es2.enter_context(nc.sbuf_tensor("f_x2", [128, NHK, TCH], F32))
            f_big = es2.enter_context(nc.sbuf_tensor("f_big", [128, NDK, TCH], BF16))
            f_ws = es2.enter_context(nc.sbuf_tensor("f_ws", [128, 2, 8, TCH], BF16))
            f_lbar = es2.enter_context(nc.sbuf_tensor("f_lbar", [32, H], BF16))
            f_laT = es2.enter_context(nc.sbuf_tensor("f_laT", [128, NHK, 32], BF16))
            f_gwT = es2.enter_context(nc.sbuf_tensor("f_gwT", [128, NHK, 4], BF16))
            f_sq = es2.enter_context(nc.sbuf_tensor("f_sq", [128, TCH], BF16))
            f_mu = es2.enter_context(nc.sbuf_tensor("f_mu", [1, TCH], F32))
            f_rsd = es2.enter_context(nc.sbuf_tensor("f_rsd", [1, TCH], F32))
            f_tmp = es2.enter_context(nc.sbuf_tensor("f_tmp", [1, TCH], F32))
            f_lg = es2.enter_context(nc.sbuf_tensor("f_lg", [4, TCH], F32))
            f_r1 = es2.enter_context(nc.sbuf_tensor("f_r1", [1, TCH], F32))
            f_r2 = es2.enter_context(nc.sbuf_tensor("f_r2", [1, TCH], F32))
            f_sqB = es2.enter_context(nc.sbuf_tensor("f_sqB", [128, TCH], BF16))
            f_oh0 = es2.enter_context(nc.sbuf_tensor("f_oh0", [1, TCH], BF16))
            f_oh1 = es2.enter_context(nc.sbuf_tensor("f_oh1", [1, TCH], BF16))
            f_oh2 = es2.enter_context(nc.sbuf_tensor("f_oh2", [1, TCH], BF16))
            f_oh3 = es2.enter_context(nc.sbuf_tensor("f_oh3", [1, TCH], BF16))
            f_ohr = es2.enter_context(nc.sbuf_tensor("f_ohr", [32, TCH], BF16))
            f_zg = es2.enter_context(nc.sbuf_tensor("f_zg", [32, TCH], BF16))
            f_of = es2.enter_context(nc.sbuf_tensor("f_of", [128, 4, TCH], F32))
            f_ost4 = es2.enter_context(nc.sbuf_tensor("f_ost4", [128, 2, TCH], F32))
            xn2 = f_big[:]
            FWR = {'par': 0, 'last': [0, 0]}
            f_stg = f_of[:].rearrange("p a t -> p (a t)")

            def f_res_ld(q):
                q.dma_start(out=f_stg[:, 0:NHK * 4], in_=d_gwT[:, :]).then_inc(sQL, 16)
                q.dma_start(out=f_stg[:, 128:128 + NHK * 32], in_=d_laT[:, :]).then_inc(sQL, 16)
                q.dma_start(out=f_stg[0:32, 1280:1280 + 768], in_=d_lbar[:, 0:768]).then_inc(sQL, 16)
            block.sync(f_res_ld)
            tfr = ctr.add("sQL", 48)

            def f_res_cv(v):
                v.wait_ge(sQL, tfr)
                v.tensor_copy(out=f_gwT[:].rearrange("p a e -> p (a e)"), in_=f_stg[:, 0:NHK * 4])
                v.tensor_copy(out=f_laT[:].rearrange("p a e -> p (a e)"), in_=f_stg[:, 128:128 + NHK * 32])
                inc(v.tensor_copy(out=f_lbar[:, 0:768], in_=f_stg[0:32, 1280:1280 + 768]), "sV")
            block.vector(f_res_cv)
            for off in range(768, H, 768):
                w = min(768, H - off)
                def ld(q, off=off, w=w):
                    q.wait_ge(sV, ctr.get("sV"))
                    q.dma_start(out=f_stg[0:32, 0:w], in_=d_lbar[:, off:off + w]).then_inc(sQL, 16)
                block.sync(ld)
                t = ctr.add("sQL", 16)
                def cv(v, off=off, w=w, t=t):
                    v.wait_ge(sQL, t)
                    inc(v.tensor_copy(out=f_lbar[:, off:off + w], in_=f_stg[0:32, 0:w]), "sV")
                block.vector(cv)

            for t2 in range(2):
                def yld(q, t2=t2):
                    q.wait_ge(sCC, T_P1CC)
                    q.wait_ge(sPE, ctr.get("sPE"))
                    for hh in range(2):
                        q.dma_start(out=f_big[:, 32 * hh:32 * (hh + 1), :],
                                    in_=d_yrc[DIN * t2:DIN * (t2 + 1), :].rearrange("(a p) t -> p a t", p=128)[
                                        :, 32 * hh:32 * (hh + 1), :]).then_inc(sQL, 16)
                block.sync(yld)
                t_yl = ctr.add("sQL", 32)

                def xld(q, t2=t2):
                    q.wait_ge(sV, ctr.get("sV"))
                    for hh in range(4):
                        q.dma_start(out=f_x2[:, 8 * hh:8 * (hh + 1), :],
                                    in_=d_xTs.rearrange("(a p) t -> p a t", p=128)[
                                        :, 8 * hh:8 * (hh + 1), 512 * t2:512 * (t2 + 1)]).then_inc(sQL, 16)
                block.sync(xld)
                t_xl = ctr.add("sQL", 64)

                for pog in range(8):
                    for kq in range(8):
                        par = FWR['par']
                        def wol(q, pog=pog, kq=kq, par=par, t2=t2,
                                t_free=FWR['last'][FWR['par']]):
                            q.wait_ge(sCC, T_P0CC)
                            q.wait_ge(sPE, t_free)
                            q.dma_start(out=f_ws[:, par, :, :],
                                        in_=d_woTb.rearrange("(a p) h -> p a h", p=128)[
                                            :, 8 * kq:8 * (kq + 1),
                                            512 * pog:512 * (pog + 1)]).then_inc(sQL, 16)
                        block.sync(wol)
                        t_wo = ctr.add("sQL", 16)

                        def omm(t, pog=pog, kq=kq, par=par, t_wo=t_wo, t2=t2):
                            t.wait_ge(sQL, max(t_wo, t_yl))
                            if kq == 0:
                                t.wait_ge(sV, ctr.get("sV"))
                            i = None
                            for k in range(8):
                                kk = 8 * kq + k
                                for poi in range(4):
                                    i = t.matmul(PSG[poi][:],
                                                 f_ws[:, par, k, :].rearrange("p (x c) -> p x c", x=4)[:, poi, :],
                                                 f_big[:, kk, :],
                                                 start=(kk == 0), stop=(kk == 63))
                            inc(i, "sPE")
                        block.tensor(omm)
                        FWR['last'][par] = ctr.get("sPE")
                        FWR['par'] = 1 - par
                    t_om = ctr.get("sPE")

                    def oev(v, pog=pog, t_om=t_om, t2=t2):
                        v.wait_ge(sPE, t_om)
                        v.wait_ge(sQL, t_xl)
                        i = None
                        for poi in range(4):
                            po = 4 * pog + poi
                            i = v.tensor_tensor(out=f_x2[:, po, :], in0=PSG[poi][:],
                                                in1=f_x2[:, po, :], op=ALU.add)
                        inc(i, "sV")
                    block.vector(oev)
                t_x2d = ctr.get("sV")
                if t2 == 0:
                    dump("yrc", d_yrc[0:2048, :], "sCC", T_P1CC)
                    dump("x2", f_x2[:], "sV", t_x2d)

                def l2call(v, t2=t2):
                    v.wait_ge(sV, t_x2d)
                    v.wait_ge(sPE, ctr.get("sPE"))
                    inc(v.tensor_copy(out=xn2[:, 0:NHK, :].rearrange("p a t -> p (a t)"),
                                      in_=f_x2[:].rearrange("p a t -> p (a t)")), "sV")
                block.vector(l2call)
                t_c2 = ctr.get("sV")
                sqpar = {'v': 0, 'last': [0, 0]}
                for k in range(NHK):
                    par = sqpar['v']
                    sqt = f_sq if par == 0 else f_sqB
                    def l2s(s, k=k, sqt=sqt, t2=t2, t_free=sqpar['last'][par]):
                        if k == 0:
                            s.wait_ge(sV, t_c2)
                        s.wait_ge(sPE, t_free)
                        inc(s.activation(out=sqt[:], in_=f_x2[:, k, :], func=AF.Square), "sA")
                    block.scalar(l2s)
                    t_s2 = ctr.get("sA")
                    def l2m(t, k=k, sqt=sqt, t_s2=t_s2, t2=t2):
                        if k == 0:
                            t.wait_ge(sV, t_c2)
                        t.matmul(ps_s1[0:1, :], r_ones_b[:], xn2[:, k, :],
                                 start=(k == 0), stop=(k == NHK - 1))
                        t.wait_ge(sA, t_s2)
                        i = t.matmul(ps_s2[0:1, :], r_ones_b[:], sqt[:],
                                     start=(k == 0), stop=(k == NHK - 1))
                        inc(i, "sPE")
                    block.tensor(l2m)
                    sqpar['last'][par] = ctr.get("sPE")
                    sqpar['v'] = 1 - par
                t_st2 = ctr.get("sPE")

                def l2f1(v, t2=t2):
                    v.wait_ge(sPE, t_st2)
                    v.tensor_scalar(out=f_mu[:], in0=ps_s1[0:1, :], scalar1=1.0 / H, scalar2=None, op0=ALU.mult)
                    inc(v.tensor_scalar(out=f_tmp[:], in0=ps_s2[0:1, :], scalar1=1.0 / H, scalar2=None, op0=ALU.mult), "sV")
                block.vector(l2f1)
                t_g1 = ctr.get("sV")

                def l2f2(s, t2=t2):
                    s.wait_ge(sV, t_g1)
                    inc(s.activation(out=f_rsd[:], in_=f_mu[:], func=AF.Square), "sA")
                block.scalar(l2f2)
                t_g2 = ctr.get("sA")

                def l2f3(v, t2=t2):
                    v.wait_ge(sA, t_g2)
                    v.tensor_tensor(out=f_tmp[:], in0=f_tmp[:], in1=f_rsd[:], op=ALU.subtract)
                    inc(v.tensor_scalar(out=f_mu[:], in0=f_mu[:], scalar1=-1.0, scalar2=None, op0=ALU.mult), "sV")
                block.vector(l2f3)
                t_g3 = ctr.get("sV")

                def l2f4(s, t2=t2):
                    s.wait_ge(sV, t_g3)
                    s.activation(out=f_tmp[:], in_=f_tmp[:], func=AF.Ln, bias=EPS_LN)
                    inc(s.activation(out=f_rsd[:], in_=f_tmp[:], func=AF.Exp, scale=-0.5), "sA")
                block.scalar(l2f4)
                t_g4 = ctr.get("sA")

                def l2bc(t, t2=t2):
                    t.wait_ge(sA, t_g4)
                    t.matmul(ps_bc[:], r_onesr_f[:], f_mu[:], start=True, stop=True)
                    inc(t.matmul(ps_x[:], r_onesr_f[:], f_rsd[:], start=True, stop=True), "sPE")
                block.tensor(l2bc)
                t_g5 = ctr.get("sPE")

                def l2n(v, t2=t2):
                    v.wait_ge(sPE, t_g5)
                    for hh in range(4):
                        sl = slice(8 * hh, 8 * (hh + 1))
                        v.tensor_tensor(out=f_x2[:, sl, :], in0=f_x2[:, sl, :],
                                        in1=ps_bc[:].unsqueeze(1).broadcast_to((128, 8, TCH)), op=ALU.add)
                        v.tensor_tensor(out=xn2[:, sl, :], in0=f_x2[:, sl, :],
                                        in1=ps_x[:].unsqueeze(1).broadcast_to((128, 8, TCH)), op=ALU.mult)
                    i = None
                    for hh in range(4):
                        sl = slice(8 * hh, 8 * (hh + 1))
                        i = v.tensor_tensor(out=f_x2[:, sl, :], in0=f_x2[:, sl, :],
                                            in1=ps_bc[:].unsqueeze(1).broadcast_to((128, 8, TCH)),
                                            op=ALU.subtract)
                    inc(i, "sV")
                block.vector(l2n)
                t_n2 = ctr.get("sV")
                if t2 == 0:
                    dump("xn2", xn2[:, 0:NHK, :], "sV", t_n2)

                def gmm(t, t2=t2):
                    t.wait_ge(sV, t_n2)
                    i = None
                    for k in range(NHK):
                        i = t.matmul(ps_s1[0:4, :], f_gwT[:, k, :], xn2[:, k, :],
                                     start=(k == 0), stop=(k == NHK - 1))
                    inc(i, "sPE")
                    i = None
                    for k in range(NHK):
                        i = t.matmul(ps_s2[0:32, :], f_laT[:, k, :], xn2[:, k, :],
                                     start=(k == 0), stop=(k == NHK - 1))
                    inc(i, "sPE")
                block.tensor(gmm)
                t_gl = ctr.get("sPE")

                def glg(v, t2=t2):
                    v.wait_ge(sPE, t_gl)
                    inc(v.tensor_scalar(out=f_lg[:], in0=ps_s1[0:4, :], scalar1=r_eb[:, 0:1],
                                        scalar2=None, op0=ALU.add), "sV")
                block.vector(glg)
                t_lg = ctr.get("sV")

                def grw(q, t2=t2):
                    q.wait_ge(sV, t_lg)
                    q.dma_start(out=f_r1[:], in_=f_lg[1:2, :]).then_inc(sQL, 16)
                    q.dma_start(out=f_r2[:], in_=f_lg[2:3, :]).then_inc(sQL, 16)
                    q.dma_start(out=f_tmp[:], in_=f_lg[3:4, :]).then_inc(sQL, 16)
                block.sync(grw)
                t_gr = ctr.add("sQL", 48)

                def gmx(v, t2=t2):
                    v.wait_ge(sQL, t_gr)
                    v.tensor_tensor(out=f_mu[:], in0=f_r1[:], in1=f_lg[0:1, :], op=ALU.max)
                    v.tensor_tensor(out=f_rsd[:], in0=f_r2[:], in1=f_tmp[:], op=ALU.max)
                    v.tensor_tensor(out=f_mu[:], in0=f_mu[:], in1=f_rsd[:], op=ALU.max)
                    v.tensor_tensor(out=f_oh0[:], in0=f_lg[0:1, :], in1=f_mu[:], op=ALU.is_ge)
                    v.tensor_tensor(out=f_oh1[:], in0=f_r1[:], in1=f_mu[:], op=ALU.is_ge)
                    v.tensor_tensor(out=f_oh2[:], in0=f_r2[:], in1=f_mu[:], op=ALU.is_ge)
                    inc(v.tensor_tensor(out=f_oh3[:], in0=f_tmp[:], in1=f_mu[:], op=ALU.is_ge), "sV")
                block.vector(gmx)
                t_oh = ctr.get("sV")
                if t2 == 0:
                    dump("lg", f_lg[:], "sV", t_oh)
                    dump("oh0", f_oh0[:], "sV", t_oh)
                    dump("oh1", f_oh1[:], "sV", t_oh)

                def ohr(q, t2=t2):
                    q.wait_ge(sV, t_oh)
                    for e, oh in enumerate((f_oh0, f_oh1, f_oh2, f_oh3)):
                        q.dma_start(out=d_ohs[e:e + 1, :], in_=oh[:]).then_inc(sQL, 16)
                block.sync(ohr)
                t_ohr0 = ctr.add("sQL", 64)

                def ohr2(q, t2=t2):
                    q.wait_ge(sQL, t_ohr0)
                    q.dma_start(out=f_ohr[:],
                                in_=d_ohs[:, :].unsqueeze(1).broadcast_to((4, 8, TCH))).then_inc(sQL, 16)
                block.sync(ohr2)
                t_ohr = ctr.add("sQL", 16)

                def zg(v, t2=t2):
                    v.wait_ge(sQL, t_ohr)
                    inc(v.tensor_tensor(out=f_zg[:], in0=ps_s2[0:32, :], in1=f_ohr[:], op=ALU.mult), "sV")
                block.vector(zg)
                t_zg = ctr.get("sV")
                if t2 == 0:
                    dump("zg", f_zg[:], "sV", t_zg)

                for pog in range(8):
                    for kq in range(4):
                        par = FWR['par']
                        def wul(q, pog=pog, kq=kq, par=par, t2=t2,
                                t_free=FWR['last'][FWR['par']]):
                            q.wait_ge(sPE, t_free)
                            q.dma_start(out=f_ws[:, par, :, :],
                                        in_=d_wuTb.rearrange("(a p) h -> p a h", p=128)[
                                            :, 8 * kq:8 * (kq + 1),
                                            512 * pog:512 * (pog + 1)]).then_inc(sQL, 16)
                        block.sync(wul)
                        t_wu = ctr.add("sQL", 16)

                        def fmm(t, pog=pog, kq=kq, par=par, t_wu=t_wu, t2=t2):
                            t.wait_ge(sQL, t_wu)
                            if kq == 0:
                                t.wait_ge(sV, ctr.get("sV"))
                            i = None
                            for k in range(8):
                                kk = 8 * kq + k
                                for poi in range(4):
                                    i = t.matmul(PSG[poi][:],
                                                 f_ws[:, par, k, :].rearrange("p (x c) -> p x c", x=4)[:, poi, :],
                                                 xn2[:, kk, :], start=(kk == 0), stop=False)
                            if kq == 3:
                                for poi in range(4):
                                    po = 4 * pog + poi
                                    i = t.matmul(PSG[poi][:], f_lbar[:, 128 * po:128 * (po + 1)], f_zg[:],
                                                 start=False, stop=True)
                            inc(i, "sPE")
                        block.tensor(fmm)
                        FWR['last'][par] = ctr.get("sPE")
                        FWR['par'] = 1 - par
                    t_fm = ctr.get("sPE")

                    def fev(v, pog=pog, t_fm=t_fm, t2=t2):
                        v.wait_ge(sPE, t_fm)
                        i = None
                        for poi in range(4):
                            po = 4 * pog + poi
                            i = v.tensor_tensor(out=f_of[:, poi, :], in0=PSG[poi][:],
                                                in1=f_x2[:, po, :], op=ALU.add)
                        inc(i, "sV")
                    block.vector(fev)
                    t_fe = ctr.get("sV")

                    for q4 in range(4):
                        def ftr(t, pog=pog, q4=q4, t_fe=t_fe, t2=t2):
                            t.wait_ge(sV, t_fe)
                            t.wait_ge(sV, ctr.get("sV"))
                            i = None
                            for poi in range(4):
                                i = t.transpose(ps_x[:, 128 * poi:128 * (poi + 1)],
                                                f_of[:, poi, 128 * q4:128 * (q4 + 1)], r_idf[:])
                            inc(i, "sPE")
                        block.tensor(ftr)
                        t_ft = ctr.get("sPE")

                        def fevc(v, t_ft=t_ft, pog=pog, q4=q4, t2=t2):
                            v.wait_ge(sPE, t_ft)
                            if q4 % 2 == 0:
                                v.wait_ge(sST, ctr.get("sST"))
                            inc(v.tensor_copy(out=f_ost4[:, q4 % 2, :], in_=ps_x[:]), "sV")
                        block.vector(fevc)
                        if q4 % 2 == 1:
                            t_fv = ctr.get("sV")
                            def fst(q, pog=pog, q4=q4, t2=t2, t_fv=t_fv):
                                q.wait_ge(sV, t_fv)
                                base = 512 * t2 + 128 * (q4 - 1)
                                q.dma_start(out=d_out[base:base + 256, 512 * pog:512 * (pog + 1)].rearrange(
                                                "(q p) c -> p q c", p=128),
                                            in_=f_ost4[:]).then_inc(sST, 16)
                            block.sync(fst)
                            ctr.add("sST", 16)

        T_ST = ctr.get("sST")

        def fin(q):
            q.wait_ge(sST, T_ST)
        block.sync(fin)

    return nc


def _rearr_pd(x, X):
    return np.ascontiguousarray(x.reshape(NDT, 128, X).transpose(1, 0, 2).reshape(128, NDT * X))


def kernel(**inputs):
    hs = np.ascontiguousarray(np.asarray(inputs["L0_attn_hidden_states"]).reshape(B * L, H))
    win = np.asarray(inputs["L0_attn_in_proj_w"])
    convw = np.asarray(inputs["L0_attn_conv_w"])
    convb = np.asarray(inputs["L0_attn_conv_b"])
    xpw = np.asarray(inputs["L0_attn_x_proj_w"])
    dtpw = np.asarray(inputs["L0_attn_dt_proj_w"])
    dtbias = np.asarray(inputs["L0_attn_dt_bias"])
    alog = np.asarray(inputs["L0_attn_A_log"])
    dpar = np.asarray(inputs["L0_attn_D_param"])
    woutw = np.asarray(inputs["L0_attn_out_proj_w"])
    dtlnw = np.asarray(inputs["L0_attn_dt_ln_w"])
    blnw = np.asarray(inputs["L0_attn_B_ln_w"])
    clnw = np.asarray(inputs["L0_attn_C_ln_w"])
    wup = np.asarray(inputs["L0_ffn_W_up"])
    gatew = np.asarray(inputs["L0_ffn_gate_w"])
    ebias = np.asarray(inputs["L0_ffn_expert_bias"])
    loraA = np.asarray(inputs["L0_ffn_lora_A"])
    loraB = np.asarray(inputs["L0_ffn_lora_B"])

    xT = np.ascontiguousarray(hs.T)
    woutT = np.ascontiguousarray(woutw.T)
    wupT = np.ascontiguousarray(wup.T)
    gwT = np.ascontiguousarray(gatew.T.reshape(NHK, 128, 4).transpose(1, 0, 2).reshape(128, NHK * 4))
    laT_full = np.ascontiguousarray(loraA.reshape(32, H).T)
    laT = np.ascontiguousarray(laT_full.reshape(NHK, 128, 32).transpose(1, 0, 2).reshape(128, NHK * 32))
    lbar = np.ascontiguousarray(loraB.transpose(0, 2, 1).reshape(32, H))

    in_maps = []
    for c in range(NC):
        b, g = c // 4, c % 4
        ds = slice(DG * g, DG * (g + 1))
        m = dict(
            xTb=xT[:, b * L:(b + 1) * L],
            xTs=np.concatenate([xT[:, 512 * c:512 * (c + 1)],
                                xT[:, L + 512 * c:L + 512 * (c + 1)]], axis=1),
            win=np.concatenate([win[:, DG * g:DG * (g + 1)],
                                win[:, DIN + DG * g:DIN + DG * (g + 1)]], axis=1),
            cw=_rearr_pd(convw[ds], KCV),
            cb=_rearr_pd(convb[ds, None], 1),
            dtb=_rearr_pd(dtbias[ds, None], 1),
            dpar=_rearr_pd(dpar[ds, None], 1),
            alog=_rearr_pd(alog[ds], NST),
            xpwT=_rearr_pd(np.ascontiguousarray(xpw[:, ds].T), ESSM),
            dtpwT=np.ascontiguousarray(
                dtpw[ds].T.reshape(2, 128, DG).transpose(1, 0, 2).reshape(128, 2 * DG)),
            dtlnw=np.ascontiguousarray(dtlnw.reshape(2, 128).T),
            blnw=blnw[:, None],
            clnw=clnw[:, None],
            woT=woutT[1024 * c:1024 * (c + 1)],
            wuT=wupT[512 * c:512 * (c + 1)],
            gwT=gwT,
            laT=laT,
            lbar=lbar,
            eb=ebias[:, None],
        )
        in_maps.append({k: np.ascontiguousarray(v, dtype=np.float32) for k, v in m.items()})

    nc = build_program()
    res = run_bass_kernel_spmd(nc, in_maps, core_ids=list(range(NC)))

    out = np.empty((B * L, H), dtype=np.float32)
    for c in range(NC):
        out[512 * c:512 * (c + 1)] = res.results[c]["out"][0:512]
        out[L + 512 * c:L + 512 * (c + 1)] = res.results[c]["out"][512:1024]
    return out.reshape(B, L, H)
